# version 06
# baseline (speedup 1.0000x reference)
import contextlib
import numpy as np
import ml_dtypes
import concourse.bass as bass
import concourse.mybir as mybir
from concourse.bass_utils import run_bass_kernel_spmd

F32 = mybir.dt.float32
BF16 = mybir.dt.bfloat16
AF = mybir.ActivationFunctionType
ALU = mybir.AluOpType
AX = mybir.AxisListType

T = 4096
DM = 2048
NH = 16
DH = 128
OWN = 1024
NT = 8
SBK = 256
NSB = T // SBK
NE = 64
I2 = 1024
II = 512
SI = 512
TB = 512

ENGINES = ("sync", "tensor", "vector", "scalar", "gpsimd")
KEYS = ("sync", "tensor", "vector", "scalar", "gpsimd", "scalard")
UNDER = {"scalard": "scalar"}
DMA_KEYS = ("sync", "gpsimd", "scalard")


class Plan:

    def __init__(self):
        self.steps = {e: [] for e in KEYS}
        self.count = {e: 0 for e in KEYS}
        self._barrier_pending = {e: [] for e in KEYS}
        self.phase = "init"
        self.seq = 0

    def step(self, eng, fn, waits=(), n_dma=0):
        w = []
        for t in list(waits) + self._barrier_pending[eng]:
            if t is not None and t[1] > 0:
                w.append(t)
        self._barrier_pending[eng] = []
        if eng in DMA_KEYS:
            inc = 16 * n_dma
        else:
            inc = 1
        self.count[eng] += inc
        tok = (eng, self.count[eng])
        self.seq += 1
        self.steps[eng].append((fn, w, tok, n_dma, self.phase, self.seq))
        return tok

    def barrier(self):
        toks = [(e, self.count[e]) for e in KEYS if self.count[e] > 0]
        for e in KEYS:
            u = UNDER.get(e, e)
            self._barrier_pending[e] = [t for t in toks
                                        if UNDER.get(t[0], t[0]) != u]

    def tok(self, eng):
        return (eng, self.count[eng])

    def merged(self, engine):
        out = []
        for k in KEYS:
            if UNDER.get(k, k) == engine:
                out.extend((s, k) for s in self.steps[k])
        out.sort(key=lambda sk: sk[0][5])
        return out

    def check(self):
        queues = {e: self.merged(e) for e in ENGINES}
        pos = {e: 0 for e in ENGINES}
        done = {e: 0 for e in KEYS}
        progress = True
        while progress:
            progress = False
            for e in ENGINES:
                while pos[e] < len(queues[e]):
                    (fn, w, tok, _, _, _), key = queues[e][pos[e]]
                    if all(done[s] >= v for (s, v) in w):
                        done[key] = tok[1]
                        pos[e] += 1
                        progress = True
                    else:
                        break
        stuck = {e: len(queues[e]) - pos[e] for e in ENGINES if pos[e] < len(queues[e])}
        if stuck:
            detail = {}
            for e in stuck:
                (fn, w, _, _, _, _), key = queues[e][pos[e]]
                detail[e] = [(s, v, done[s]) for (s, v) in w if done[s] < v]
            raise RuntimeError(f"plan deadlock: stuck={stuck} blocking={detail}")

    def emit(self, nc, sems):
        plan = self

        def runner(eng_name):
            def run(eobj):
                watermark = {}
                cur_phase = [None]
                import contextlib as _ctx
                scope = [_ctx.nullcontext()]
                for (fn, waits, tok, n_dma, phase, _seq), key in plan.merged(eng_name):
                    if phase != cur_phase[0]:
                        scope[0].__exit__(None, None, None)
                        scope[0] = nc.named_scope(phase)
                        scope[0].__enter__()
                        cur_phase[0] = phase
                    for (s, v) in waits:
                        if watermark.get(s, -1) >= v:
                            continue
                        eobj.wait_ge(sems[s], v)
                        watermark[s] = v
                    insts = fn(eobj)
                    if insts is None:
                        insts = []
                    if not isinstance(insts, (list, tuple)):
                        insts = [insts]
                    if key in DMA_KEYS:
                        if n_dma == 0:
                            continue
                        assert len(insts) == n_dma, (key, len(insts), n_dma)
                        for ins in insts:
                            ins.then_inc(sems[key], 16)
                    else:
                        assert len(insts) >= 1
                        insts[-1].then_inc(sems[key], 1)
                scope[0].__exit__(None, None, None)
            return run

        with nc.Block() as block:
            block.sync(runner("sync"))
            block.tensor(runner("tensor"))
            block.vector(runner("vector"))
            block.scalar(runner("scalar"))
            block.gpsimd(runner("gpsimd"))


def segs_for(qoff):
    if qoff < 512:
        return [(qoff, 512), (512, 1024)]
    return [(qoff, 1024)]


def build_program(dbg=False, serial_attn=False):
    nc = bass.Bass("TRN2", target_bir_lowering=False)
    dt = nc.dram_tensor

    x_feat = dt("x_feat", [DM, T], F32, kind="ExternalInput").ap()
    x_own_feat = dt("x_own_feat", [DM, OWN], F32, kind="ExternalInput").ap()
    x_own = dt("x_own", [OWN, DM], F32, kind="ExternalInput").ap()
    cosk = dt("cosk", [64, T], F32, kind="ExternalInput").ap()
    sink = dt("sink", [64, T], F32, kind="ExternalInput").ap()
    cosq = dt("cosq", [64, OWN], F32, kind="ExternalInput").ap()
    sinq = dt("sinq", [64, OWN], F32, kind="ExternalInput").ap()
    masks = dt("masks", [32 * 128, 128], BF16, kind="ExternalInput").ap()
    wq = dt("wq", [DM, NH * DH], F32, kind="ExternalInput").ap()
    wk = dt("wk", [DM, NH * DH], F32, kind="ExternalInput").ap()
    wv = dt("wv", [DM, NH * DH], F32, kind="ExternalInput").ap()
    wo = dt("wo", [NH, DH, DM], F32, kind="ExternalInput").ap()
    gate_wT = dt("gate_wT", [DM, NE], F32, kind="ExternalInput").ap()
    gu_wT = dt("gu_wT", [NE, DM, I2], F32, kind="ExternalInput").ap()
    dp_wT = dt("dp_wT", [NE, II, DM], F32, kind="ExternalInput").ap()
    shgT = dt("shgT", [DM, SI], F32, kind="ExternalInput").ap()
    shuT = dt("shuT", [DM, SI], F32, kind="ExternalInput").ap()
    shdT = dt("shdT", [SI, DM], F32, kind="ExternalInput").ap()
    g1h = dt("g1h", [128, 16], F32, kind="ExternalInput").ap()
    c_ident = dt("c_ident", [128, 128], BF16, kind="ExternalInput").ap()
    c_ones_row_bf = dt("c_ones_row_bf", [1, 128], BF16, kind="ExternalInput").ap()
    c_ones_row_f32 = dt("c_ones_row_f32", [1, 128], F32, kind="ExternalInput").ap()
    c_ones_col_bf = dt("c_ones_col_bf", [128, 1], BF16, kind="ExternalInput").ap()

    out = dt("out", [OWN, DM], F32, kind="ExternalOutput").ap()

    Kscr = dt("Kscr", [NH, DH, T], BF16, kind="Internal").ap()
    Vscr = dt("Vscr", [NH, T, DH], BF16, kind="Internal").ap()
    XLscr = dt("XLscr", [DM, T], BF16, kind="Internal").ap()
    XL2scr = dt("XL2scr", [DM, OWN], BF16, kind="Internal").ap()
    X1scr = dt("X1scr", [OWN, DM], F32, kind="Internal").ap()
    WTscr = dt("WTscr", [NE, OWN], BF16, kind="Internal").ap()

    if dbg:
        dbg_xl = dt("dbg_xl", [DM, T], BF16, kind="ExternalOutput").ap()
        dbg_k = dt("dbg_k", [NH, DH, T], BF16, kind="ExternalOutput").ap()
        dbg_v = dt("dbg_v", [NH, T, DH], BF16, kind="ExternalOutput").ap()
        dbg_q = dt("dbg_q", [128, NH * OWN], BF16, kind="ExternalOutput").ap()
        dbg_ot = dt("dbg_ot", [128, NH * OWN], BF16, kind="ExternalOutput").ap()
        dbg_rr = dt("dbg_rr", [NH, OWN], F32, kind="ExternalOutput").ap()
        dbg_br = dt("dbg_br", [NH, OWN], F32, kind="ExternalOutput").ap()
        dbg_sp = dt("dbg_sp", [4, 128, 1024], F32, kind="ExternalOutput").ap()
        dbg_pp = dt("dbg_pp", [4, 128, 1024], BF16, kind="ExternalOutput").ap()
        dbg_x1s = dt("dbg_x1s", [OWN, DM], F32, kind="ExternalOutput").ap()
        dbg_sg = dt("dbg_sg", [2, 128, 4, 512], F32, kind="ExternalOutput").ap()
        dbg_shint = dt("dbg_shint", [2, 128, 4, 512], BF16, kind="ExternalOutput").ap()
        dbg_shg = dt("dbg_shg", [2, 128, 4], F32, kind="ExternalOutput").ap()
        dbg_shgb = dt("dbg_shgb", [128, 16, 512], BF16, kind="ExternalOutput").ap()
        dbg_x1 = dt("dbg_x1", [OWN, DM], F32, kind="ExternalOutput").ap()
        dbg_xl2 = dt("dbg_xl2", [DM, OWN], BF16, kind="ExternalOutput").ap()
        dbg_wt = dt("dbg_wt", [NE, OWN], BF16, kind="ExternalOutput").ap()

    xfr = x_feat.rearrange("(c p) t -> p c t", p=128)
    xofr = x_own_feat.rearrange("(c p) t -> p c t", p=128)
    xlr = XLscr.rearrange("(c p) t -> p c t", p=128)
    xl2r = XL2scr.rearrange("(c p) t -> p c t", p=128)
    wqr = wq.rearrange("(c p) n -> p c n", p=128)
    wkr = wk.rearrange("(c p) n -> p c n", p=128)
    wvr = wv.rearrange("(c p) n -> p c n", p=128)
    gwr = gate_wT.rearrange("(c p) n -> p c n", p=128)
    shgr = shgT.rearrange("(c p) n -> p c n", p=128)
    shur = shuT.rearrange("(c p) n -> p c n", p=128)
    shdr = shdT.rearrange("(c p) n -> p c n", p=128)
    maskr = masks.rearrange("(c p) q -> p c q", p=128)

    P = Plan()

    gstack = contextlib.ExitStack()
    qt_stack = contextlib.ExitStack()
    qt = qt_stack.enter_context(nc.sbuf_tensor("qt", [128, NH, OWN], BF16, side="right"))
    ident = gstack.enter_context(nc.sbuf_tensor("ident", [128, 128], BF16))
    ones_row_bf = gstack.enter_context(nc.sbuf_tensor("ones_row_bf", [1, 128], BF16))
    ones_row_f32 = gstack.enter_context(nc.sbuf_tensor("ones_row_f32", [1, 128], F32))
    ones_col_bf = gstack.enter_context(nc.sbuf_tensor("ones_col_bf", [128, 1], BF16))

    tok_consts = P.step("sync", lambda e: [
        e.dma_start(ident[:], c_ident),
        e.dma_start(ones_row_bf[:], c_ones_row_bf),
        e.dma_start(ones_row_f32[:], c_ones_row_f32),
        e.dma_start(ones_col_bf[:], c_ones_col_bf),
    ], n_dma=4)

    P.phase = "phA"
    xlo_stack = contextlib.ExitStack()
    xlo = xlo_stack.enter_context(nc.sbuf_tensor("xlo", [128, NH, OWN], BF16, side="right"))

    with contextlib.ExitStack() as sa:
        xst = [sa.enter_context(nc.sbuf_tensor(f"xst{i}", [128, 16, SBK], F32)) for i in range(2)]
        xbf = sa.enter_context(nc.sbuf_tensor("xbf", [128, 16, SBK], BF16))
        xsq = sa.enter_context(nc.sbuf_tensor("xsq", [128, 16, SBK], BF16))
        xln = [sa.enter_context(nc.sbuf_tensor(f"xln{i}", [128, 16, SBK], BF16)) for i in range(2)]
        ta = sa.enter_context(nc.sbuf_tensor("ta", [128, SBK], F32))
        tb_ = sa.enter_context(nc.sbuf_tensor("tb_", [128, SBK], F32))
        rstd_sb = sa.enter_context(nc.sbuf_tensor("rstd_sb", [128, SBK], F32))
        mrs_sb = sa.enter_context(nc.sbuf_tensor("mrs_sb", [128, SBK], F32))
        cext = [sa.enter_context(nc.sbuf_tensor(f"cext{i}", [128, SBK], F32)) for i in range(2)]
        sext = [sa.enter_context(nc.sbuf_tensor(f"sext{i}", [128, SBK], F32)) for i in range(2)]
        kqo = [sa.enter_context(nc.sbuf_tensor(f"kqo{i}", [128, SBK], BF16)) for i in range(3)]
        vev = [sa.enter_context(nc.sbuf_tensor(f"vev{i}", [128, 512], BF16)) for i in range(2)]
        rows = sa.enter_context(nc.sbuf_tensor("rows", [1, 3, SBK], F32))

        with contextlib.ExitStack() as spa:
            ps_s = spa.enter_context(nc.psum_tensor("ps_s", [1, SBK], F32))
            ps_s2 = spa.enter_context(nc.psum_tensor("ps_s2", [1, SBK], F32))
            ps_bc = spa.enter_context(nc.psum_tensor("ps_bc", [128, 2 * SBK], F32))
            psK = [spa.enter_context(nc.psum_tensor(f"psK{i}", [128, SBK], F32)) for i in range(3)]
            psV = [spa.enter_context(nc.psum_tensor(f"psV{i}", [128, 512], F32)) for i in range(2)]

            with contextlib.ExitStack() as swa:
                wkb = [swa.enter_context(nc.sbuf_tensor(f"wkb{i}", [128, 16, 512], BF16))
                       for i in range(2)]
                wvb = [swa.enter_context(nc.sbuf_tensor(f"wvb{i}", [128, 16, 512], BF16))
                       for i in range(2)]

                st = {
                    "x_used": [tok_consts, tok_consts],
                    "xbf_used": None,
                    "xln_used": [[], []],
                    "rope_last": [None, None],
                    "kqo_used": [None, None, None],
                    "vev_used": [None, None],
                    "prev_trow": None,
                    "prev_sq": None,
                    "prev_bc": None,
                    "prev_norm": None,
                    "kidx": 0,
                }

                def ln_block(src3d, blk_off, into, par):
                    tok_dma = P.step("sync", lambda e: e.dma_start(
                        xst[par][:], src3d[:, :, blk_off:blk_off + SBK]),
                        waits=[st["x_used"][par]], n_dma=1)
                    w = [tok_dma]
                    if st["xbf_used"]:
                        w.append(st["xbf_used"])
                    tcast = P.step("vector", lambda e, par=par: [
                        e.tensor_copy(xbf[:, kc, :], xst[par][:, kc, :]) for kc in range(16)
                    ], waits=w)
                    w = [tcast]
                    if st["prev_sq"]:
                        w.append(st["prev_sq"])
                    tsq = P.step("scalar", lambda e: [
                        e.activation(xsq[:, kc, :], xbf[:, kc, :], AF.Square)
                        for kc in range(16)
                    ], waits=w)
                    w = [tcast, tok_consts]
                    if st["prev_trow"]:
                        w.append(st["prev_trow"])
                    tmm1 = P.step("tensor", lambda e: [
                        e.matmul(ps_s[:], ones_col_bf[:], xbf[:, kc, :],
                                 start=(kc == 0), stop=(kc == 15), skip_group_check=True)
                        for kc in range(16)
                    ], waits=w)
                    st["xbf_used"] = tmm1
                    w = [tsq]
                    if st["prev_trow"]:
                        w.append(st["prev_trow"])
                    tmm2 = P.step("tensor", lambda e: [
                        e.matmul(ps_s2[:], ones_col_bf[:], xsq[:, kc, :],
                                 start=(kc == 0), stop=(kc == 15), skip_group_check=True)
                        for kc in range(16)
                    ], waits=w)
                    st["prev_sq"] = tmm2
                    trow = P.step("vector", lambda e: [
                        e.tensor_scalar(rows[0:1, 0, :], ps_s[:], 1.0 / DM, None, ALU.mult),
                        e.tensor_scalar(rows[0:1, 1, :], ps_s2[:], 1.0 / DM, None, ALU.mult),
                        e.tensor_tensor(rows[0:1, 2, :], rows[0:1, 0, :], rows[0:1, 0, :], ALU.mult),
                    ], waits=[tmm1, tmm2])
                    st["prev_trow"] = trow
                    trow2 = P.step("vector", lambda e: e.scalar_tensor_tensor(
                        rows[0:1, 2, :], rows[0:1, 1, :], 1e-5, rows[0:1, 2, :],
                        ALU.add, ALU.subtract), waits=[trow])
                    tr1 = P.step("scalar", lambda e: e.activation(
                        rows[0:1, 1, :], rows[0:1, 2, :], AF.Ln), waits=[trow2])
                    tr2 = P.step("scalar", lambda e: e.activation(
                        rows[0:1, 1, :], rows[0:1, 1, :], AF.Exp, scale=-0.5), waits=[tr1])
                    tr3 = P.step("vector", lambda e: e.tensor_tensor(
                        rows[0:1, 0, :], rows[0:1, 0, :], rows[0:1, 1, :], ALU.mult), waits=[tr2])
                    w = [tr2, tr3]
                    if st["prev_bc"]:
                        w.append(st["prev_bc"])
                    tbc = P.step("tensor", lambda e: [
                        e.matmul(ps_bc[:, 0:SBK], ones_row_f32[:], rows[0:1, 1, :],
                                 start=True, stop=True, skip_group_check=True),
                        e.matmul(ps_bc[:, SBK:2 * SBK], ones_row_f32[:], rows[0:1, 0, :],
                                 start=True, stop=True, skip_group_check=True),
                    ], waits=w)
                    w = [tbc]
                    if st["prev_norm"]:
                        w.append(st["prev_norm"])
                    tevc = P.step("vector", lambda e: [
                        e.tensor_copy(rstd_sb[:], ps_bc[:, 0:SBK]),
                        e.tensor_copy(mrs_sb[:], ps_bc[:, SBK:2 * SBK]),
                    ], waits=w)
                    st["prev_bc"] = tevc
                    ops = []
                    w = [tevc] + st["xln_used"][into]
                    def norm_fn(e, par=par, into=into):
                        out = []
                        for kc in range(16):
                            out.append(e.tensor_tensor(ta[:, 0:SBK], xst[par][:, kc, :],
                                                       rstd_sb[:], ALU.mult))
                            out.append(e.tensor_tensor(xln[into][:, kc, :], ta[:, 0:SBK],
                                                       mrs_sb[:], ALU.subtract))
                        return out
                    tnorm = P.step("vector", norm_fn, waits=w)
                    st["xln_used"][into] = []
                    st["x_used"][par] = tnorm
                    st["prev_norm"] = tnorm
                    return tnorm

                xlo_done = []
                for blk in range(OWN // SBK):
                    par = blk % 2
                    tnorm = ln_block(xofr, blk * SBK, 0, par)
                    tcp = P.step("vector", lambda e, blk=blk: [
                        e.tensor_copy(xlo[:, kc, blk * SBK:(blk + 1) * SBK], xln[0][:, kc, :])
                        for kc in range(16)
                    ], waits=[tnorm])
                    st["xln_used"][0] = [tcp]
                    xlo_done = [tcp]

                def prep(sb, half):
                    into = par = sb % 2
                    t0 = sb * SBK
                    if half == 0:
                        tnorm = ln_block(xfr, t0, into, par)
                        txw = P.step("sync", lambda e, t0=t0, into=into: e.dma_start(
                            xlr[:, :, t0:t0 + SBK], xln[into][:]), waits=[tnorm], n_dma=1)
                        st["xln_used"][into].append(txw)
                        return tnorm
                    tok = P.step("sync", lambda e, t0=t0, into=into: e.dma_start(
                        xln[into][:], xlr[:, :, t0:t0 + SBK]),
                        waits=st["xln_used"][into], n_dma=1)
                    st["xln_used"][into] = []
                    return tok

                def kv_block(sb, h0, into, xln_ready, twk, ppar):
                    t0 = sb * SBK
                    par = sb % 2
                    wct = [st["rope_last"][par]] if st["rope_last"][par] else []
                    tct = P.step("scalard", lambda e, t0=t0, par=par: [
                        e.dma_start(cext[par][0:64, :], cosk[:, t0:t0 + SBK]),
                        e.dma_start(cext[par][64:128, :], cosk[:, t0:t0 + SBK]),
                        e.dma_start(sext[par][0:64, :], sink[:, t0:t0 + SBK]),
                        e.dma_start(sext[par][64:128, :], sink[:, t0:t0 + SBK]),
                    ], waits=wct, n_dma=4)
                    for hh in range(4):
                        h = h0 + hh
                        ki = st["kidx"] % 3
                        st["kidx"] += 1
                        wmm = [xln_ready, twk]
                        if st["kqo_used"][ki]:
                            wmm.append(st["kqo_used"][ki])
                        tmm = P.step("tensor", lambda e, hh=hh, into=into, ki=ki, ppar=ppar: [
                            e.matmul(psK[ki][:], wkb[ppar][:, kc, hh * DH:(hh + 1) * DH],
                                     xln[into][:, kc, :], start=(kc == 0), stop=(kc == 15),
                                     skip_group_check=True)
                            for kc in range(16)
                        ], waits=wmm)
                        trp = P.step("vector", lambda e, ki=ki, par=par: [
                            e.tensor_tensor(ta[:, 0:SBK], psK[ki][:], cext[par][:], ALU.mult),
                            e.tensor_tensor(tb_[0:64, 0:SBK], psK[ki][64:128, :],
                                            sext[par][64:128, :], ALU.mult),
                            e.tensor_tensor(tb_[64:128, 0:SBK], psK[ki][0:64, :],
                                            sext[par][0:64, :], ALU.mult),
                            e.tensor_tensor(kqo[ki][0:64, :], ta[0:64, 0:SBK],
                                            tb_[0:64, 0:SBK], ALU.subtract),
                            e.tensor_tensor(kqo[ki][64:128, :], ta[64:128, 0:SBK],
                                            tb_[64:128, 0:SBK], ALU.add),
                        ], waits=[tmm, tct])
                        st["rope_last"][par] = trp
                        tkw = P.step("scalard", lambda e, h=h, t0=t0, ki=ki: e.dma_start(
                            Kscr[h, :, t0:t0 + SBK], kqo[ki][:]), waits=[trp], n_dma=1)
                        st["kqo_used"][ki] = tkw
                    for hg in range(1):
                        for tt in range(2):
                            vpar = tt % 2
                            wmm = [xln_ready, twk]
                            if st["vev_used"][vpar]:
                                wmm.append(st["vev_used"][vpar])
                            tmm = P.step("tensor", lambda e, tt=tt, into=into, vpar=vpar, ppar=ppar: [
                                e.matmul(psV[vpar][:], xln[into][:, kc, tt * 128:(tt + 1) * 128],
                                         wvb[ppar][:, kc, :],
                                         start=(kc == 0), stop=(kc == 15), skip_group_check=True)
                                for kc in range(16)
                            ], waits=wmm)
                            tev = P.step("scalar", lambda e, vpar=vpar: e.activation(
                                vev[vpar][:], psV[vpar][:], AF.Copy), waits=[tmm])
                            tvw = P.step("scalard", lambda e, h0=h0, tt=tt, t0=t0, vpar=vpar: [
                                e.dma_start(
                                    Vscr[h0 + hh2,
                                         t0 + tt * 128:t0 + tt * 128 + 128, :],
                                    vev[vpar][:, hh2 * 128:(hh2 + 1) * 128])
                                for hh2 in range(4)
                            ], waits=[tev], n_dma=4)
                            st["vev_used"][vpar] = tvw
                    st["xln_used"][into].append(P.tok("tensor"))

                wpass_used = [None, None]
                twk_all = {}

                def load_pass_weights(q):
                    ppar = q % 2
                    h0 = q * 4
                    w = [t for t in [wpass_used[ppar]] if t]
                    twk_all[q] = P.step("gpsimd", lambda e, h0=h0, ppar=ppar: [
                        e.dma_start(wkb[ppar][:], wkr[:, :, h0 * DH:(h0 + 4) * DH]),
                        e.dma_start(wvb[ppar][:], wvr[:, :, h0 * DH:(h0 + 4) * DH]),
                    ], waits=w, n_dma=2)

                load_pass_weights(0)
                load_pass_weights(1)
                for qtr in range(4):
                    h0 = qtr * 4
                    ppar = qtr % 2
                    ready = {0: prep(0, qtr)}
                    for sb in range(NSB):
                        if sb + 1 < NSB:
                            ready[sb + 1] = prep(sb + 1, qtr)
                        kv_block(sb, h0, sb % 2, ready[sb], twk_all[qtr], ppar)
                    wpass_used[ppar] = P.tok("tensor")
                    if qtr + 2 < 4:
                        load_pass_weights(qtr + 2)

        P.phase = "phQ"
        with contextlib.ExitStack() as sq_:
            psQ = [sq_.enter_context(nc.psum_tensor(f"psQ{i}", [128, 512], F32)) for i in range(2)]
            wqb = sq_.enter_context(nc.sbuf_tensor("wqb", [128, 16, 1024], BF16))
            cqe = sq_.enter_context(nc.sbuf_tensor("cqe", [128, OWN], F32))
            sqe = sq_.enter_context(nc.sbuf_tensor("sqe", [128, OWN], F32))
            qta = sq_.enter_context(nc.sbuf_tensor("qta", [128, 512], F32))
            qtb = sq_.enter_context(nc.sbuf_tensor("qtb", [128, 512], F32))
            tcq = P.step("sync", lambda e: [
                e.dma_start(cqe[0:64, :], cosq),
                e.dma_start(cqe[64:128, :], cosq),
                e.dma_start(sqe[0:64, :], sinq),
                e.dma_start(sqe[64:128, :], sinq),
            ], n_dma=4)
            rope_done = [None, None]
            for half in range(2):
                h0 = half * 8
                twq = P.step("gpsimd", lambda e, h0=h0: e.dma_start(
                    wqb[:], wqr[:, :, h0 * DH:(h0 + 8) * DH]),
                    waits=[P.tok("tensor")], n_dma=1)
                for hh in range(8):
                    h = h0 + hh
                    for qb in range(2):
                        qpar = (hh * 2 + qb) % 2
                        w = [twq, tcq] + xlo_done
                        if rope_done[qpar]:
                            w.append(rope_done[qpar])
                        tmm = P.step("tensor", lambda e, hh=hh, qb=qb, qpar=qpar: [
                            e.matmul(psQ[qpar][:], wqb[:, kc, hh * DH:(hh + 1) * DH],
                                     xlo[:, kc, qb * 512:(qb + 1) * 512],
                                     start=(kc == 0), stop=(kc == 15), skip_group_check=True)
                            for kc in range(16)
                        ], waits=w)
                        trp = P.step("vector", lambda e, h=h, qb=qb, qpar=qpar: [
                            e.tensor_tensor(qta[:], psQ[qpar][:], cqe[:, qb * 512:(qb + 1) * 512], ALU.mult),
                            e.tensor_tensor(qtb[0:64, :], psQ[qpar][64:128, :],
                                            sqe[64:128, qb * 512:(qb + 1) * 512], ALU.mult),
                            e.tensor_tensor(qtb[64:128, :], psQ[qpar][0:64, :],
                                            sqe[0:64, qb * 512:(qb + 1) * 512], ALU.mult),
                            e.tensor_tensor(qt[0:64, h, qb * 512:(qb + 1) * 512], qta[0:64, :],
                                            qtb[0:64, :], ALU.subtract),
                            e.tensor_tensor(qt[64:128, h, qb * 512:(qb + 1) * 512], qta[64:128, :],
                                            qtb[64:128, :], ALU.add),
                        ], waits=[tmm])
                        rope_done[qpar] = trp

    xlo_stack.close()
    P.barrier()

    if dbg:
        P.step("sync", lambda e: [
            e.dma_start(dbg_xl, XLscr),
            e.dma_start(dbg_k, Kscr),
            e.dma_start(dbg_v, Vscr),
            e.dma_start(dbg_q.rearrange("p (h q) -> p h q", h=NH), qt[:]),
        ], n_dma=4)
        P.barrier()

    P.phase = "phB"
    ot_stack = contextlib.ExitStack()
    ot = ot_stack.enter_context(nc.sbuf_tensor("ot", [128, NH, OWN], BF16, side="right"))

    with contextlib.ExitStack() as sb_:
        masks_sb = sb_.enter_context(nc.sbuf_tensor("masks_sb", [128, 32, 128], BF16))
        kbuf = sb_.enter_context(nc.sbuf_tensor("kbuf", [128, 2, T], BF16))
        vbuf = sb_.enter_context(nc.sbuf_tensor("vbuf", [128, 2, 32, DH], BF16))
        enbp = [sb_.enter_context(nc.sbuf_tensor(f"enbp{i}", [128, 1024], BF16)) for i in range(2)]
        bmm01 = [sb_.enter_context(nc.sbuf_tensor(f"bmm01{i}", [128, 32, 128], BF16)) for i in range(2)]
        pexp = [sb_.enter_context(nc.sbuf_tensor(f"pe{i}", [128, 1024], BF16)) for i in range(2)]
        pbuf = [sb_.enter_context(nc.sbuf_tensor(f"pb{i}", [128, 1024], BF16)) for i in range(2)]
        k2 = sb_.enter_context(nc.sbuf_tensor("k2", [128, T], BF16))
        q2 = sb_.enter_context(nc.sbuf_tensor("q2", [128, OWN], BF16))
        rkm8 = sb_.enter_context(nc.sbuf_tensor("rkm8", [1, 8], F32))
        rkmax = sb_.enter_context(nc.sbuf_tensor("rkmax", [1, 1], F32))
        rq2 = sb_.enter_context(nc.sbuf_tensor("rq2", [1, 1024], F32))
        brow = sb_.enter_context(nc.sbuf_tensor("brow", [1, 1024], F32))
        brow_bf = sb_.enter_context(nc.sbuf_tensor("brow_bf", [1, 1024], BF16))
        rrow = sb_.enter_context(nc.sbuf_tensor("rrow", [1, 1024], F32))
        rbc = sb_.enter_context(nc.sbuf_tensor("rbc", [128, 1024], F32))

        with contextlib.ExitStack() as spb:
            psS = [spb.enter_context(nc.psum_tensor(f"psS{i}", [128, 1024], F32)) for i in range(2)]
            psO = spb.enter_context(nc.psum_tensor("psO", [128, 1024], F32))
            psSum = spb.enter_context(nc.psum_tensor("psSum", [1, 1024], F32))

            tmask = P.step("sync", lambda e: e.dma_start(masks_sb[:], maskr), n_dma=1)
            kv_loaded = [None, None]
            kv_freed = [None, None]
            prev_tOev = None
            prev_trc = None
            prev_enb_used = [None, None]

            kv_loaded[0] = P.step("sync", lambda e: [
                e.dma_start(kbuf[:, 0, :], Kscr[0]),
                e.dma_start(vbuf[:, 0, :, :], Vscr[0].rearrange("(c p) d -> p c d", p=128)),
            ], n_dma=2)

            for h in range(NH):
                par = h % 2
                if h + 1 < NH:
                    w = [kv_freed[1 - par]] if kv_freed[1 - par] else []
                    kv_loaded[1 - par] = P.step("sync", lambda e, h=h, par=par: [
                        e.dma_start(kbuf[:, 1 - par, :], Kscr[h + 1]),
                        e.dma_start(vbuf[:, 1 - par, :, :],
                                    Vscr[h + 1].rearrange("(c p) d -> p c d", p=128)),
                    ], waits=w, n_dma=2)

                tk2 = P.step("scalar", lambda e, par=par: [
                    e.activation(k2[:, i * 1024:(i + 1) * 1024],
                                 kbuf[:, par, i * 1024:(i + 1) * 1024], AF.Square)
                    for i in range(4)
                ], waits=[kv_loaded[par]])
                tq2 = P.step("scalar", lambda e, h=h: e.activation(q2[:], qt[:, h, :], AF.Square))
                prev_ev = prev_trc
                rk_toks = []
                for ch in range(8):
                    seg = ch % 2
                    w = [tk2, tok_consts] + ([prev_ev] if prev_ev else [])
                    tmmr = P.step("tensor", lambda e, ch=ch, seg=seg: e.matmul(
                        psSum[0:1, seg * 512:(seg + 1) * 512], ones_col_bf[:],
                        k2[:, ch * 512:(ch + 1) * 512], start=True, stop=True,
                        skip_group_check=True), waits=w)
                    tev = P.step("vector", lambda e, ch=ch, seg=seg: e.reduce_max(
                        rkm8[0:1, ch:ch + 1], psSum[0:1, seg * 512:(seg + 1) * 512], axis=AX.X),
                        waits=[tmmr])
                    prev_ev = tev
                    rk_toks.append(tev)
                tqr = []
                for seg in range(2):
                    tmmq = P.step("tensor", lambda e, seg=seg: e.matmul(
                        psSum[0:1, seg * 512:(seg + 1) * 512], ones_col_bf[:],
                        q2[:, seg * 512:(seg + 1) * 512], start=True, stop=True,
                        skip_group_check=True), waits=[tq2, rk_toks[-1]])
                    tevq = P.step("vector", lambda e, seg=seg: e.tensor_copy(
                        rq2[0:1, seg * 512:(seg + 1) * 512],
                        psSum[0:1, seg * 512:(seg + 1) * 512]), waits=[tmmq])
                    tqr.append(tevq)
                tmx = P.step("vector", lambda e: e.reduce_max(rkmax[:], rkm8[:], axis=AX.X),
                             waits=rk_toks)
                tbr = P.step("vector", lambda e: e.tensor_scalar(
                    brow[:], rq2[:], rkmax[:], 0.5, ALU.add, ALU.mult), waits=[tmx] + tqr)
                tbr2 = P.step("vector", lambda e: e.tensor_copy(brow_bf[:], brow[:]), waits=[tbr])
                w = [tbr2] + ([prev_tOev] if prev_tOev else [])
                tbc = P.step("tensor", lambda e: [
                    e.matmul(psO[:, s * 512:(s + 1) * 512], ones_row_bf[:],
                             brow_bf[0:1, s * 512:(s + 1) * 512], start=True, stop=True,
                             skip_group_check=True)
                    for s in range(2)
                ], waits=w)
                hpar = h % 2
                w = [tbc]
                if prev_enb_used[hpar]:
                    w.append(prev_enb_used[hpar])
                tenb = P.step("scalar", lambda e, hpar=hpar: e.activation(
                    enbp[hpar][:], psO[:], AF.Exp, scale=-1.0), waits=w)
                tprep = P.step("vector", lambda e, hpar=hpar: [
                    e.tensor_tensor(bmm01[hpar][:, jc, :],
                                    enbp[hpar][:, (jc // 4) * 128:(jc // 4) * 128 + 128],
                                    masks_sb[:, jc, :], ALU.mult) for jc in range(32)
                ], waits=[tenb, tmask])

                exp_toks = [prev_tOev, prev_tOev]
                mul_toks = [None, None]
                omm_toks = [None, None]
                s_tok = {}

                def s_step(jc):
                    spar = jc % 2
                    qoff = 128 * (jc // 4)
                    segs = segs_for(qoff)
                    w = [kv_loaded[par]]
                    if exp_toks[spar]:
                        w.append(exp_toks[spar])
                    s_tok[jc] = P.step("tensor", lambda e, jc=jc, spar=spar, segs=segs, par=par, h=h: [
                        e.matmul(psS[spar][:, a:b], kbuf[:, par, jc * 128:(jc + 1) * 128],
                                 qt[:, h, a:b], start=True, stop=True, skip_group_check=True)
                        for (a, b) in segs
                    ], waits=w)

                s_step(0)
                for jc in range(32):
                    if serial_attn:
                        P.barrier()
                    g = jc // 4
                    qoff = 128 * g
                    spar = jc % 2
                    segs = segs_for(qoff)
                    w = [s_tok[jc]]
                    if mul_toks[spar]:
                        w.append(mul_toks[spar])
                    texp = P.step("scalar", lambda e, qoff=qoff, spar=spar: e.activation(
                        pexp[spar][:, qoff:1024], psS[spar][:, qoff:1024], AF.Exp), waits=w)
                    exp_toks[spar] = texp
                    w = [texp, tprep]
                    if omm_toks[spar]:
                        w.append(omm_toks[spar])
                    tmul = P.step("vector", lambda e, jc=jc, qoff=qoff, spar=spar, hpar=hpar: (
                        [e.tensor_tensor(pbuf[spar][:, qoff:qoff + 128],
                                         pexp[spar][:, qoff:qoff + 128],
                                         bmm01[hpar][:, jc, :], ALU.mult)] +
                        ([e.tensor_tensor(pbuf[spar][:, qoff + 128:1024],
                                          pexp[spar][:, qoff + 128:1024],
                                          enbp[hpar][:, qoff + 128:1024], ALU.mult)]
                         if qoff + 128 < 1024 else [])
                    ), waits=w)
                    mul_toks[spar] = tmul
                    if serial_attn:
                        P.barrier()
                    if jc + 1 < 32:
                        s_step(jc + 1)
                    if serial_attn:
                        P.barrier()
                    tO = P.step("tensor", lambda e, jc=jc, spar=spar, segs=segs, par=par: [
                        e.matmul(psO[:, a:b], vbuf[:, par, jc, :], pbuf[spar][:, a:b],
                                 start=(jc == 0), stop=(jc == 31), skip_group_check=True)
                        for (a, b) in segs
                    ], waits=[tmul, tenb])
                    w = [tmul]
                    if jc == 0:
                        w += tqr
                    tSum = P.step("tensor", lambda e, jc=jc, spar=spar, segs=segs: [
                        e.matmul(psSum[0:1, a:b], ones_col_bf[:], pbuf[spar][:, a:b],
                                 start=(jc == 0), stop=(jc == 31), skip_group_check=True)
                        for (a, b) in segs
                    ], waits=w)
                    omm_toks[spar] = tSum
                kv_freed[par] = P.tok("tensor")
                prev_enb_used[hpar] = mul_toks[1]

                trc = P.step("vector", lambda e: e.reciprocal(rrow[:], psSum[:]),
                             waits=[omm_toks[0], omm_toks[1]])
                prev_trc = trc
                if dbg:
                    prev_trc = P.step("sync", lambda e, h=h: [
                        e.dma_start(dbg_rr[h][None, :], rrow[:]),
                        e.dma_start(dbg_br[h][None, :], brow[:]),
                    ], waits=[trc, tbr], n_dma=2)
                tbc2 = P.step("tensor", lambda e: [
                    e.matmul(psS[0][:, s * 512:(s + 1) * 512], ones_row_f32[:],
                             rrow[0:1, s * 512:(s + 1) * 512], start=True, stop=True,
                             skip_group_check=True)
                    for s in range(2)
                ], waits=[trc, exp_toks[0], exp_toks[1]])
                trbc = P.step("vector", lambda e: e.tensor_copy(rbc[:], psS[0][:]), waits=[tbc2])
                tOev = P.step("vector", lambda e, h=h: e.tensor_tensor(
                    ot[:, h, :], psO[:], rbc[:], ALU.mult), waits=[trbc, P.tok("tensor")])
                prev_tOev = tOev

    P.barrier()

    if dbg:
        P.step("sync", lambda e: e.dma_start(
            dbg_ot.rearrange("p (h q) -> p h q", h=NH), ot[:]), n_dma=1)
        P.barrier()

    P.phase = "phOUT"
    with contextlib.ExitStack() as so:
        wob = so.enter_context(nc.sbuf_tensor("wob", [128, NH, DM], BF16))
        gwb = so.enter_context(nc.sbuf_tensor("gwb", [128, 16, NE], BF16))
        xot = [so.enter_context(nc.sbuf_tensor(f"xot{i}", [128, DM], F32)) for i in range(2)]
        x1t = [so.enter_context(nc.sbuf_tensor(f"x1t{i}", [128, DM], F32)) for i in range(2)]
        sqt = so.enter_context(nc.sbuf_tensor("sqt", [128, DM], F32))
        st1 = so.enter_context(nc.sbuf_tensor("st1", [128, 8], F32))
        xl2t = so.enter_context(nc.sbuf_tensor("xl2t", [128, DM], BF16))
        xl2tt = so.enter_context(nc.sbuf_tensor("xl2tt", [128, 16, 128], BF16))
        lg = so.enter_context(nc.sbuf_tensor("lg", [128, NE], F32))
        m8 = so.enter_context(nc.sbuf_tensor("m8", [128, 8], F32))
        nm1 = so.enter_context(nc.sbuf_tensor("nm1", [128, 1], F32))
        e8 = so.enter_context(nc.sbuf_tensor("e8", [128, 8], F32))
        eL = so.enter_context(nc.sbuf_tensor("eL", [128, NE], F32))
        selb = so.enter_context(nc.sbuf_tensor("selb", [128, NE], F32))
        s8 = so.enter_context(nc.sbuf_tensor("s8", [128, 1], F32))
        r8 = so.enter_context(nc.sbuf_tensor("r8", [128, 1], F32))
        wnb = so.enter_context(nc.sbuf_tensor("wnb", [128, NE], BF16))
        wtsb = so.enter_context(nc.sbuf_tensor("wtsb", [64, OWN], BF16))

        with contextlib.ExitStack() as spo:
            psR = [spo.enter_context(nc.psum_tensor(f"psR{i}", [128, 512], F32)) for i in range(2)]
            psL = spo.enter_context(nc.psum_tensor("psL", [128, NE], F32))
            psWT = spo.enter_context(nc.psum_tensor("psWT", [64, 128], BF16))
            psTr = [spo.enter_context(nc.psum_tensor(f"psTr{i}", [128, 128], BF16)) for i in range(2)]

            two = P.step("gpsimd", lambda e: [
                e.dma_start(wob[:, hh, :], wo[hh]) for hh in range(NH)
            ], n_dma=NH)
            tgw = P.step("gpsimd", lambda e: e.dma_start(gwb[:], gwr), n_dma=1)

            xo_tok = [None, None]
            xo_used = [None, None]
            prev_ev = [None, None]
            prev_trev = [None, None]
            prev_wtev = None
            prev_lg = None
            prev_teL = None
            prev_x1_readers = [[], []]
            ev_done = {}
            prev_tst = [None]

            def stage1(tt):
                par = tt % 2
                w = [xo_used[par]] if xo_used[par] else []
                xo_tok[par] = P.step("sync", lambda e, tt=tt, par=par: e.dma_start(
                    xot[par][:], x_own[tt * 128:(tt + 1) * 128, :]), waits=w, n_dma=1)
                ev_toks = []
                for mb in range(4):
                    rpar = mb % 2
                    w = [two]
                    if prev_ev[rpar]:
                        w.append(prev_ev[rpar])
                    tmm = P.step("tensor", lambda e, tt=tt, mb=mb, rpar=rpar: [
                        e.matmul(psR[rpar][:], ot[:, hh, tt * 128:(tt + 1) * 128],
                                 wob[:, hh, mb * 512:(mb + 1) * 512],
                                 start=(hh == 0), stop=(hh == NH - 1), skip_group_check=True)
                        for hh in range(NH)
                    ], waits=w)
                    tev = P.step("vector", lambda e, mb=mb, rpar=rpar, par=par: e.tensor_tensor(
                        x1t[par][:, mb * 512:(mb + 1) * 512], psR[rpar][:],
                        xot[par][:, mb * 512:(mb + 1) * 512], ALU.add),
                        waits=[tmm, xo_tok[par]] + (prev_x1_readers[par] if mb == 0 else []))
                    prev_ev[rpar] = tev
                    ev_toks.append(tev)
                xo_used[par] = ev_toks[-1]
                ev_done[tt] = ev_toks

            def stage2(tt):
                nonlocal prev_wtev, prev_lg, prev_teL
                par = tt % 2
                ev_toks = ev_done[tt]
                tx1w = P.step("sync", lambda e, tt=tt, par=par: e.dma_start(
                    X1scr[tt * 128:(tt + 1) * 128, :], x1t[par][:]), waits=ev_toks, n_dma=1)
                w = list(ev_toks)
                if prev_tst[0]:
                    w.append(prev_tst[0])
                tsq = P.step("scalar", lambda e, par=par: e.activation(
                    sqt[:], x1t[par][:], AF.Square), waits=w)
                tst = P.step("vector", lambda e, par=par: [
                    e.reduce_sum(st1[:, 0:1], x1t[par][:], axis=AX.X),
                    e.reduce_sum(st1[:, 1:2], sqt[:], axis=AX.X),
                ], waits=[tsq] + ev_toks)
                prev_tst[0] = tst
                tst2 = P.step("vector", lambda e: [
                    e.tensor_scalar(st1[:, 2:3], st1[:, 0:1], 1.0 / DM, None, ALU.mult),
                    e.tensor_scalar(st1[:, 3:4], st1[:, 1:2], 1.0 / DM, None, ALU.mult),
                ], waits=[tst])
                tst3 = P.step("vector", lambda e: [
                    e.tensor_tensor(st1[:, 4:5], st1[:, 2:3], st1[:, 2:3], ALU.mult),
                ], waits=[tst2])
                tst4 = P.step("vector", lambda e: [
                    e.scalar_tensor_tensor(st1[:, 5:6], st1[:, 3:4], 1e-5, st1[:, 4:5],
                                           ALU.add, ALU.subtract),
                ], waits=[tst3])
                tr1 = P.step("scalar", lambda e: e.activation(
                    st1[:, 6:7], st1[:, 5:6], AF.Ln), waits=[tst4])
                tr2 = P.step("scalar", lambda e: e.activation(
                    st1[:, 6:7], st1[:, 6:7], AF.Exp, scale=-0.5), waits=[tr1])
                w = [tr2, tst2, tx1w]
                if prev_trev[0]:
                    w += [prev_trev[0], prev_trev[1]]
                tnorm = P.step("vector", lambda e, par=par: e.tensor_scalar(
                    xl2t[:], x1t[par][:], st1[:, 2:3], st1[:, 6:7], ALU.subtract, ALU.mult),
                    waits=w)
                prev_x1_readers[par] = [tnorm, tx1w]
                tr_toks = []
                for mc in range(16):
                    tpar = mc % 2
                    w = [tnorm, tok_consts]
                    if prev_trev[tpar]:
                        w.append(prev_trev[tpar])
                    ttr = P.step("tensor", lambda e, mc=mc, tpar=tpar: e.transpose(
                        psTr[tpar][:], xl2t[:, mc * 128:(mc + 1) * 128], ident[:]), waits=w)
                    tte = P.step("vector", lambda e, mc=mc, tpar=tpar: e.tensor_copy(
                        xl2tt[:, mc, :], psTr[tpar][:]), waits=[ttr])
                    prev_trev[tpar] = tte
                    tr_toks.append(tte)
                txl2w = P.step("sync", lambda e, tt=tt: e.dma_start(
                    xl2r[:, :, tt * 128:(tt + 1) * 128], xl2tt[:]), waits=tr_toks, n_dma=1)
                w = [tgw] + tr_toks
                if prev_lg:
                    w.append(prev_lg)
                tlg = P.step("tensor", lambda e: [
                    e.matmul(psL[:], xl2tt[:, kc, :], gwb[:, kc, :],
                             start=(kc == 0), stop=(kc == 15), skip_group_check=True)
                    for kc in range(16)
                ], waits=w)
                tlgc = P.step("vector", lambda e: e.tensor_copy(lg[:], psL[:]), waits=[tlg])
                prev_lg = tlgc
                tm8 = P.step("vector", lambda e: e.max(m8[:], lg[:]), waits=[tlgc])
                w = [tm8]
                if prev_teL:
                    w.append(prev_teL)
                tnm = P.step("vector", lambda e: e.tensor_scalar(
                    nm1[:], m8[:, 0:1], -1.0, None, ALU.mult), waits=w)
                te8 = P.step("scalar", lambda e: e.activation(e8[:], m8[:], AF.Exp, bias=nm1[:]),
                             waits=[tnm])
                teL = P.step("scalar", lambda e: e.activation(eL[:], lg[:], AF.Exp, bias=nm1[:]),
                             waits=[tnm, tlgc])
                prev_teL = teL
                ts8 = P.step("vector", lambda e: e.reduce_sum(s8[:], e8[:], axis=AX.X), waits=[te8])
                tr8 = P.step("vector", lambda e: e.reciprocal(r8[:], s8[:]), waits=[ts8])
                tsel = P.step("vector", lambda e: e.tensor_single_scalar(
                    selb[:], lg[:], m8[:, 7:8], ALU.is_ge), waits=[tm8])
                tw_ = P.step("vector", lambda e: e.tensor_tensor(
                    selb[:], eL[:], selb[:], ALU.mult), waits=[tsel, teL])
                twn = P.step("vector", lambda e: e.tensor_scalar(
                    wnb[:], selb[:], r8[:], None, ALU.mult), waits=[tw_, tr8])
                w = [twn, tok_consts]
                if prev_wtev:
                    w.append(prev_wtev)
                ttw = P.step("tensor", lambda e: e.transpose(psWT[:], wnb[:], ident[:]), waits=w)
                twte = P.step("vector", lambda e, tt=tt: e.tensor_copy(
                    wtsb[:, tt * 128:(tt + 1) * 128], psWT[:]), waits=[ttw])
                prev_wtev = twte

            stage1(0)
            for tt in range(NT):
                if tt + 1 < NT:
                    stage1(tt + 1)
                stage2(tt)
            P.step("sync", lambda e: e.dma_start(WTscr, wtsb[:]), waits=[prev_wtev], n_dma=1)

    ot_stack.close()
    qt_stack.close()
    P.barrier()

    if dbg:
        P.step("sync", lambda e: [
            e.dma_start(dbg_x1, X1scr),
            e.dma_start(dbg_xl2, XL2scr),
            e.dma_start(dbg_wt, WTscr),
        ], n_dma=3)
        P.barrier()

    P.phase = "phSH"
    gub_stack = contextlib.ExitStack()
    qbuf = [gub_stack.enter_context(nc.sbuf_tensor(f"qbuf{i}", [128, 16, 256], BF16, side="right"))
            for i in range(3)]

    def gu_q_ap(ex, q4):
        return gu_wT[ex].rearrange("(c p) i -> p c i", p=128)[:, :, q4 * 256:(q4 + 1) * 256]

    qtok = [None, None, None]
    qfree = [None, None, None]
    for qi in range(2):
        qtok[qi] = P.step("gpsimd", lambda e, qi=qi: e.dma_start(
            qbuf[qi][:], gu_q_ap(0, qi)), n_dma=1)

    x1rr = X1scr.rearrange("(c p) m -> p c m", p=128)

    def x1_block_ap(t0):
        return X1scr[t0:t0 + TB, :].rearrange("(c p) m -> p c m", p=128)

    def out_block_ap(t0):
        return out[t0:t0 + TB, :].rearrange("(c p) m -> p c m", p=128)

    with contextlib.ExitStack() as ss:
        shgb = ss.enter_context(nc.sbuf_tensor("shgb", [128, 16, SI], BF16))
        shub = ss.enter_context(nc.sbuf_tensor("shub", [128, 16, SI], BF16))
        shdb = ss.enter_context(nc.sbuf_tensor("shdb", [128, 4, DM], BF16))
        g1b = ss.enter_context(nc.sbuf_tensor("g1b", [128, 16], BF16))
        xl2b = ss.enter_context(nc.sbuf_tensor("xl2b", [128, 16, TB], BF16))
        x1b = ss.enter_context(nc.sbuf_tensor("x1b", [128, 4, DM], F32))
        sg = ss.enter_context(nc.sbuf_tensor("sg", [128, 4, SI], F32))
        shint = ss.enter_context(nc.sbuf_tensor("shint", [128, 4, TB], BF16))
        shg = ss.enter_context(nc.sbuf_tensor("shg", [128, 4], F32))

        with contextlib.ExitStack() as sps:
            psA = [sps.enter_context(nc.psum_tensor(f"psSA{i}", [128, 512], F32)) for i in range(2)]
            psD = [sps.enter_context(nc.psum_tensor(f"psSD{i}", [128, 512], F32)) for i in range(2)]
            psG = sps.enter_context(nc.psum_tensor("psG", [128, 4], F32))

            tshw = P.step("gpsimd", lambda e: [
                e.dma_start(shgb[:], shgr),
                e.dma_start(shub[:], shur),
                e.dma_start(shdb[:], shdr),
                e.dma_start(g1b[:], g1h),
            ], n_dma=4)
            if dbg:
                tshw = P.step("sync", lambda e: e.dma_start(dbg_shgb, shgb[:]),
                              waits=[tshw], n_dma=1)

            prev_wr = None
            for tbk in range(2):
                t0 = tbk * TB
                w = [prev_wr] if prev_wr else []
                tx1 = P.step("sync", lambda e, t0=t0: [
                    e.dma_start(x1b[:], x1_block_ap(t0)),
                    e.dma_start(xl2b[:], xl2r[:, :, t0:t0 + TB]),
                ], waits=w, n_dma=2)
                prev_g = [None, None]
                sg_toks = []
                for sc in range(4):
                    gpar = sc % 2
                    w = [tx1, tshw]
                    if prev_g[gpar]:
                        w.append(prev_g[gpar])
                    tg = P.step("tensor", lambda e, sc=sc, gpar=gpar: [
                        e.matmul(psA[gpar][:], shgb[:, kc, sc * 128:(sc + 1) * 128],
                                 xl2b[:, kc, :], start=(kc == 0), stop=(kc == 15),
                                 skip_group_check=True)
                        for kc in range(16)
                    ], waits=w)
                    tsg = P.step("scalar", lambda e, sc=sc, gpar=gpar: e.activation(
                        sg[:, sc, :], psA[gpar][:], AF.Silu), waits=[tg])
                    prev_g[gpar] = tsg
                    sg_toks.append(tsg)
                up_toks = []
                for sc in range(4):
                    gpar = sc % 2
                    tu = P.step("tensor", lambda e, sc=sc, gpar=gpar: [
                        e.matmul(psA[gpar][:], shub[:, kc, sc * 128:(sc + 1) * 128],
                                 xl2b[:, kc, :], start=(kc == 0), stop=(kc == 15),
                                 skip_group_check=True)
                        for kc in range(16)
                    ], waits=[tx1, prev_g[gpar]])
                    tui = P.step("vector", lambda e, sc=sc, gpar=gpar: e.tensor_tensor(
                        shint[:, sc, :], psA[gpar][:], sg[:, sc, :], ALU.mult),
                        waits=[tu, sg_toks[sc]])
                    prev_g[gpar] = tui
                    up_toks.append(tui)
                tgl = P.step("tensor", lambda e: [
                    e.matmul(psG[:, ts_:ts_ + 1], xl2b[:, kc, ts_ * 128:(ts_ + 1) * 128],
                             g1b[:, kc:kc+1], start=(kc == 0 and ts_ == 0),
                             stop=(kc == 15 and ts_ == 3),
                             skip_group_check=True)
                    for ts_ in range(4) for kc in range(16)
                ], waits=[tx1, tshw])
                tsig = P.step("scalar", lambda e: e.activation(shg[:], psG[:], AF.Sigmoid),
                              waits=[tgl])
                if dbg:
                    P.barrier()
                    P.step("sync", lambda e, tbk=tbk: [
                        e.dma_start(dbg_sg[tbk], sg[:]),
                        e.dma_start(dbg_shint[tbk], shint[:]),
                        e.dma_start(dbg_shg[tbk], shg[:]),
                    ], n_dma=3)
                    P.barrier()
                prev_d = [up_toks[-1], up_toks[-1]]
                dn_toks = []
                for ts_ in range(4):
                    for hb in range(4):
                        dpar = (ts_ * 4 + hb) % 2
                        td = P.step("tensor", lambda e, ts_=ts_, hb=hb, dpar=dpar: [
                            e.matmul(psD[dpar][:], shint[:, sc, ts_ * 128:(ts_ + 1) * 128],
                                     shdb[:, sc, hb * 512:(hb + 1) * 512],
                                     start=(sc == 0), stop=(sc == 3), skip_group_check=True)
                            for sc in range(4)
                        ], waits=up_toks + [prev_d[dpar]])
                        tda = P.step("vector", lambda e, ts_=ts_, hb=hb, dpar=dpar:
                                     e.scalar_tensor_tensor(
                                         x1b[:, ts_, hb * 512:(hb + 1) * 512], psD[dpar][:],
                                         shg[:, ts_:ts_ + 1],
                                         x1b[:, ts_, hb * 512:(hb + 1) * 512],
                                         ALU.mult, ALU.add),
                                     waits=[td, tsig])
                        prev_d[dpar] = tda
                        dn_toks.append(tda)
                prev_wr = P.step("sync", lambda e, t0=t0: e.dma_start(
                    x1_block_ap(t0), x1b[:]), waits=dn_toks, n_dma=1)

    P.barrier()

    if dbg:
        P.step("sync", lambda e: e.dma_start(dbg_x1s, X1scr), n_dma=1)
        P.barrier()

    P.phase = "phFFN"
    with contextlib.ExitStack() as sf:
        xl2c = sf.enter_context(nc.sbuf_tensor("xl2c", [128, 16, OWN], BF16))
        x1c = sf.enter_context(nc.sbuf_tensor("x1c", [128, 8, DM], F32))
        dpb = [sf.enter_context(nc.sbuf_tensor(f"dpb{i}", [128, 4, DM], BF16)) for i in range(2)]
        intr = [sf.enter_context(nc.sbuf_tensor(f"intr{i}", [128, 4, TB], BF16)) for i in range(4)]
        silu_g = sf.enter_context(nc.sbuf_tensor("silu_g", [128, 2, 4, TB], BF16))
        tup = sf.enter_context(nc.sbuf_tensor("tup", [128, TB], F32))
        wrow = [sf.enter_context(nc.sbuf_tensor(f"wrow{i}", [1, 2, OWN], BF16)) for i in range(2)]

        with contextlib.ExitStack() as spf:
            psGa = [spf.enter_context(nc.psum_tensor(f"psGa{i}", [128, 512], F32)) for i in range(4)]
            psWb = [spf.enter_context(nc.psum_tensor(f"psWb{i}", [128, 512], F32)) for i in range(2)]
            psDn = [spf.enter_context(nc.psum_tensor(f"psDn{i}", [128, 512], F32)) for i in range(2)]

            dp_tok = [None, None]
            down_done = {}
            upev_done = {}
            gate_done = [None, None]
            up_done = [None, None]

            def dp_ap(ex):
                return dp_wT[ex].rearrange("(c p) m -> p c m", p=128)

            tx1 = P.step("sync", lambda e: [
                e.dma_start(x1c[:], X1scr.rearrange("(c p) m -> p c m", p=128)),
                e.dma_start(xl2c[:], xl2r[:, :, :]),
            ], waits=[P.tok("vector")], n_dma=2)
            wr_tok = {}
            wr_tok[0] = P.step("sync", lambda e: [
                e.dma_start(wrow[0][0:1, i, :], WTscr[i, :][None, :]) for i in range(2)
            ], waits=[P.tok("vector")], n_dma=2)
            dp_tok[0] = P.step("gpsimd", lambda e: e.dma_start(dpb[0][:], dp_ap(0)), n_dma=1)
            dp_tok[1] = P.step("gpsimd", lambda e: e.dma_start(dpb[1][:], dp_ap(1)), n_dma=1)
            bank_ev = {}

            def down_pair(g):
                last = None
                dn_prev = [None, None]
                for tbk in range(2):
                    for ts_ in range(4):
                        tile = tbk * 4 + ts_
                        for hb in range(4):
                            dpar = (ts_ * 4 + hb) % 2
                            w = [upev_done[(2 * g) * 2 + tbk], upev_done[(2 * g + 1) * 2 + tbk],
                                 dp_tok[0], dp_tok[1]]
                            if dn_prev[dpar]:
                                w.append(dn_prev[dpar])
                            td = P.step("tensor", lambda e, g=g, tbk=tbk, ts_=ts_, hb=hb, dpar=dpar: [
                                e.matmul(psDn[dpar][:],
                                         intr[ee * 2 + tbk][:, ic, ts_ * 128:(ts_ + 1) * 128],
                                         dpb[ee][:, ic, hb * 512:(hb + 1) * 512],
                                         start=(ee == 0 and ic == 0), stop=(ee == 1 and ic == 3),
                                         skip_group_check=True)
                                for ee in range(2) for ic in range(4)
                            ], waits=w)
                            ta2 = P.step("vector", lambda e, tile=tile, hb=hb, dpar=dpar:
                                         e.tensor_tensor(
                                             x1c[:, tile, hb * 512:(hb + 1) * 512], psDn[dpar][:],
                                             x1c[:, tile, hb * 512:(hb + 1) * 512], ALU.add),
                                         waits=[td, tx1])
                            dn_prev[dpar] = ta2
                            last = ta2
                down_done[g] = last

            for g in range(NE // 2):
                if g + 1 < NE // 2:
                    wr_tok[g + 1] = P.step("sync", lambda e, g=g: [
                        e.dma_start(wrow[(g + 1) % 2][0:1, i, :], WTscr[2 * (g + 1) + i, :][None, :])
                        for i in range(2)
                    ], waits=[P.tok("tensor")], n_dma=2)
                for ee in range(2):
                    ex = 2 * g + ee
                    twb = {}
                    for tbk in range(2):
                        w = [wr_tok[g], tok_consts]
                        prev_wb = upev_done.get((ex - 1) * 2 + tbk)
                        if prev_wb:
                            w.append(prev_wb)
                        twb[tbk] = P.step("tensor", lambda e, g=g, ee=ee, tbk=tbk: e.matmul(
                            psWb[tbk][:], ones_row_bf[:],
                            wrow[g % 2][0:1, ee, tbk * TB:(tbk + 1) * TB],
                            start=True, stop=True, skip_group_check=True), waits=w)
                    silu_toks = {}
                    up_last = {0: None, 1: None}
                    for q4 in range(4):
                        slot = (ex * 4 + q4) % 3
                        nq = ex * 4 + q4 + 1
                        if nq < NE * 4:
                            nslot = nq % 3
                            wq_ = [t for t in [qfree[nslot]] if t]
                            qtok[nslot] = P.step("gpsimd", lambda e, nq=nq, nslot=nslot: e.dma_start(
                                qbuf[nslot][:], gu_q_ap(nq // 4, nq % 4)), waits=wq_, n_dma=1)
                        for sub in range(2):
                            ic = q4 * 2 + sub
                            for tbk in range(2):
                                t0 = tbk * TB
                                bank = (ic % 2) * 2 + tbk
                                w = [qtok[slot], tx1]
                                if bank_ev.get(bank):
                                    w.append(bank_ev[bank])
                                if g >= 1 and down_done.get(g - 1):
                                    w.append(down_done[g - 1])
                                tmm = P.step("tensor", lambda e, sub=sub, bank=bank, slot=slot, t0=t0: [
                                    e.matmul(psGa[bank][:], qbuf[slot][:, kc, sub * 128:(sub + 1) * 128],
                                             xl2c[:, kc, t0:t0 + TB], start=(kc == 0), stop=(kc == 15),
                                             skip_group_check=True)
                                    for kc in range(16)
                                ], waits=w)
                                qfree[slot] = tmm
                                if ic < 4:
                                    tev = P.step("scalar", lambda e, ic=ic, tbk=tbk, bank=bank:
                                                 e.activation(silu_g[:, tbk, ic, :], psGa[bank][:],
                                                              AF.Silu), waits=[tmm])
                                    silu_toks[(ic, tbk)] = tev
                                else:
                                    t1 = P.step("vector", lambda e, ic=ic, tbk=tbk, bank=bank:
                                                e.tensor_tensor(tup[:], psGa[bank][:],
                                                                silu_g[:, tbk, ic - 4, :], ALU.mult),
                                                waits=[tmm, silu_toks[(ic - 4, tbk)]])
                                    tev = P.step("vector", lambda e, ic=ic, ee=ee, tbk=tbk:
                                                 e.tensor_tensor(
                                                     intr[ee * 2 + tbk][:, ic - 4, :], tup[:],
                                                     psWb[tbk][:], ALU.mult),
                                                 waits=[t1, twb[tbk]])
                                    up_last[tbk] = tev
                                bank_ev[bank] = tev
                    upev_done[ex * 2 + 0] = up_last[0]
                    upev_done[ex * 2 + 1] = up_last[1]
                down_pair(g)
                for nxe in (2 * g + 2, 2 * g + 3):
                    if nxe < NE:
                        dp_tok[nxe % 2] = P.step("gpsimd", lambda e, nxe=nxe: e.dma_start(
                            dpb[nxe % 2][:], dp_ap(nxe)), waits=[down_done[g]], n_dma=1)
            P.step("sync", lambda e: e.dma_start(
                out.rearrange("(c p) m -> p c m", p=128), x1c[:]),
                waits=[down_done[NE // 2 - 1]], n_dma=1)

    gub_stack.close()

    P.step("sync", lambda e: e.nop(), waits=[P.tok("sync")], n_dma=0)

    P.check()

    with contextlib.ExitStack() as semstack:
        sems = {}
        for name in KEYS:
            sems[name] = semstack.enter_context(nc.semaphore(f"sem_{name}"))
        P.emit(nc, sems)

    gstack.close()
    return nc



def core_tiles(core):
    j = core % 4
    tiles = []
    for s in range(4):
        tiles.extend([8 * s + j, 8 * s + (7 - j)])
    return sorted(tiles)


def _host_inputs(core, full, consts):
    b = core // 4
    tiles = core_tiles(core)
    rows = np.concatenate([np.arange(128 * t, 128 * t + 128) for t in tiles])

    xb = full["xb"][b]
    x_own = np.ascontiguousarray(xb[rows])
    masks = np.ones((32, 128, 128), dtype=np.float32)
    jj = np.arange(128)[:, None]
    qq = np.arange(128)[None, :]
    tri = np.where(jj > qq, np.float32(0.0), np.float32(1.0))
    for s in range(8):
        t_s = tiles[s]
        for i in range(4):
            jc = 4 * s + i
            if jc == t_s:
                masks[jc] = tri
            elif jc > t_s:
                masks[jc] = np.float32(0.0)
    d = {
        "x_feat": full["x_feat"][b],
        "x_own_feat": np.ascontiguousarray(x_own.T),
        "x_own": x_own,
        "cosk": consts["cosk"], "sink": consts["sink"],
        "cosq": np.ascontiguousarray(consts["cosk"][:, rows]),
        "sinq": np.ascontiguousarray(consts["sink"][:, rows]),
        "masks": masks.reshape(32 * 128, 128).astype(ml_dtypes.bfloat16),
    }
    d.update(full["weights"])
    d.update(consts["small"])
    return d, b, rows


def prep_full(L0_attn_x, L0_attn_wq, L0_attn_wk, L0_attn_wv, L0_attn_wo,
              L0_ffn_gate_w, L0_ffn_gu_w, L0_ffn_dp_w, L0_ffn_sh_gate_w,
              L0_ffn_sh_up_w, L0_ffn_sh_down_w, L0_ffn_sh_gate1_w):
    x = np.asarray(L0_attn_x)
    full = {
        "xb": [np.ascontiguousarray(x[b]) for b in range(2)],
        "x_feat": [np.ascontiguousarray(x[b].T) for b in range(2)],
        "weights": {
            "wq": np.ascontiguousarray(np.asarray(L0_attn_wq).reshape(DM, NH * DH)),
            "wk": np.ascontiguousarray(np.asarray(L0_attn_wk).reshape(DM, NH * DH)),
            "wv": np.ascontiguousarray(np.asarray(L0_attn_wv).reshape(DM, NH * DH)),
            "wo": np.ascontiguousarray(np.asarray(L0_attn_wo)),
            "gate_wT": np.ascontiguousarray(np.asarray(L0_ffn_gate_w).T),
            "gu_wT": np.ascontiguousarray(np.asarray(L0_ffn_gu_w).transpose(0, 2, 1)),
            "dp_wT": np.ascontiguousarray(np.asarray(L0_ffn_dp_w).transpose(0, 2, 1)),
            "shgT": np.ascontiguousarray(np.asarray(L0_ffn_sh_gate_w).T),
            "shuT": np.ascontiguousarray(np.asarray(L0_ffn_sh_up_w).T),
            "shdT": np.ascontiguousarray(np.asarray(L0_ffn_sh_down_w).T),
            "g1h": np.ascontiguousarray(np.asarray(L0_ffn_sh_gate1_w).reshape(16, 128).T),
        },
    }
    half = 64
    pos = np.arange(T, dtype=np.float32)[:, None]
    freqs = np.power(np.float32(10000.0),
                     -(np.arange(half, dtype=np.float32) / np.float32(half)))[None, :]
    rad = pos * freqs
    mult = np.float32(np.sqrt(1.0 / 128.0))
    consts = {
        "cosk": np.ascontiguousarray((np.cos(rad) * mult).astype(np.float32).T),
        "sink": np.ascontiguousarray((np.sin(rad) * mult).astype(np.float32).T),
        "small": {
            "c_ident": np.eye(128, dtype=ml_dtypes.bfloat16),
            "c_ones_row_bf": np.ones((1, 128), dtype=ml_dtypes.bfloat16),
            "c_ones_row_f32": np.ones((1, 128), dtype=np.float32),
            "c_ones_col_bf": np.ones((128, 1), dtype=ml_dtypes.bfloat16),
        },
    }
    return full, consts


def kernel(**inputs):
    full, consts = prep_full(**inputs)
    in_maps = []
    row_maps = []
    for core in range(8):
        d, b, rows = _host_inputs(core, full, consts)
        in_maps.append(d)
        row_maps.append((b, rows))
    nc = build_program(dbg=False)
    res = run_bass_kernel_spmd(nc, in_maps, core_ids=list(range(8)))
    outp = np.empty((2, T, DM), dtype=np.float32)
    for core in range(8):
        b, rows = row_maps[core]
        outp[b][rows] = res.results[core]["out"]
    return outp


# version 08 (final)
# speedup vs baseline: 1.1142x; 1.1142x over previous
import contextlib
import numpy as np
import ml_dtypes
import concourse.bass as bass
import concourse.mybir as mybir
from concourse.bass_utils import run_bass_kernel_spmd

F32 = mybir.dt.float32
BF16 = mybir.dt.bfloat16
AF = mybir.ActivationFunctionType
ALU = mybir.AluOpType
AX = mybir.AxisListType

T = 4096
DM = 2048
NH = 16
DH = 128
OWN = 1024
NT = 8
SBK = 256
NSB = T // SBK
NE = 64
I2 = 1024
II = 512
SI = 512
TB = 512

ENGINES = ("sync", "tensor", "vector", "scalar", "gpsimd")
KEYS = ("sync", "tensor", "vector", "scalar", "gpsimd", "scalard")
UNDER = {"scalard": "scalar"}
DMA_KEYS = ("sync", "gpsimd", "scalard")


class Plan:

    def __init__(self):
        self.steps = {e: [] for e in KEYS}
        self.count = {e: 0 for e in KEYS}
        self._barrier_pending = {e: [] for e in KEYS}
        self.phase = "init"
        self.seq = 0

    def step(self, eng, fn, waits=(), n_dma=0):
        w = []
        for t in list(waits) + self._barrier_pending[eng]:
            if t is not None and t[1] > 0:
                w.append(t)
        self._barrier_pending[eng] = []
        if eng in DMA_KEYS:
            inc = 16 * n_dma
        else:
            inc = 1
        self.count[eng] += inc
        tok = (eng, self.count[eng])
        self.seq += 1
        self.steps[eng].append((fn, w, tok, n_dma, self.phase, self.seq))
        return tok

    def barrier(self):
        toks = [(e, self.count[e]) for e in KEYS if self.count[e] > 0]
        for e in KEYS:
            u = UNDER.get(e, e)
            self._barrier_pending[e] = [t for t in toks
                                        if UNDER.get(t[0], t[0]) != u]

    def tok(self, eng):
        return (eng, self.count[eng])

    def merged(self, engine):
        out = []
        for k in KEYS:
            if UNDER.get(k, k) == engine:
                out.extend((s, k) for s in self.steps[k])
        out.sort(key=lambda sk: sk[0][5])
        return out

    def check(self):
        queues = {e: self.merged(e) for e in ENGINES}
        pos = {e: 0 for e in ENGINES}
        done = {e: 0 for e in KEYS}
        progress = True
        while progress:
            progress = False
            for e in ENGINES:
                while pos[e] < len(queues[e]):
                    (fn, w, tok, _, _, _), key = queues[e][pos[e]]
                    if all(done[s] >= v for (s, v) in w):
                        done[key] = tok[1]
                        pos[e] += 1
                        progress = True
                    else:
                        break
        stuck = {e: len(queues[e]) - pos[e] for e in ENGINES if pos[e] < len(queues[e])}
        if stuck:
            detail = {}
            for e in stuck:
                (fn, w, _, _, _, _), key = queues[e][pos[e]]
                detail[e] = [(s, v, done[s]) for (s, v) in w if done[s] < v]
            raise RuntimeError(f"plan deadlock: stuck={stuck} blocking={detail}")

    def emit(self, nc, sems):
        plan = self

        def runner(eng_name):
            def run(eobj):
                watermark = {}
                cur_phase = [None]
                import contextlib as _ctx
                scope = [_ctx.nullcontext()]
                for (fn, waits, tok, n_dma, phase, _seq), key in plan.merged(eng_name):
                    if phase != cur_phase[0]:
                        scope[0].__exit__(None, None, None)
                        scope[0] = nc.named_scope(phase)
                        scope[0].__enter__()
                        cur_phase[0] = phase
                    for (s, v) in waits:
                        if watermark.get(s, -1) >= v:
                            continue
                        eobj.wait_ge(sems[s], v)
                        watermark[s] = v
                    insts = fn(eobj)
                    if insts is None:
                        insts = []
                    if not isinstance(insts, (list, tuple)):
                        insts = [insts]
                    if key in DMA_KEYS:
                        if n_dma == 0:
                            continue
                        assert len(insts) == n_dma, (key, len(insts), n_dma)
                        for ins in insts:
                            ins.then_inc(sems[key], 16)
                    else:
                        assert len(insts) >= 1
                        insts[-1].then_inc(sems[key], 1)
                scope[0].__exit__(None, None, None)
            return run

        with nc.Block() as block:
            block.sync(runner("sync"))
            block.tensor(runner("tensor"))
            block.vector(runner("vector"))
            block.scalar(runner("scalar"))
            block.gpsimd(runner("gpsimd"))


def segs_for(qoff):
    if qoff < 512:
        return [(qoff, 512), (512, 1024)]
    return [(qoff, 1024)]


def build_program(dbg=False, serial_attn=False):
    nc = bass.Bass("TRN2", target_bir_lowering=False)
    dt = nc.dram_tensor

    x_feat = dt("x_feat", [DM, T], F32, kind="ExternalInput").ap()
    x_own_feat = dt("x_own_feat", [DM, OWN], F32, kind="ExternalInput").ap()
    x_own = dt("x_own", [OWN, DM], F32, kind="ExternalInput").ap()
    cosk = dt("cosk", [64, T], F32, kind="ExternalInput").ap()
    sink = dt("sink", [64, T], F32, kind="ExternalInput").ap()
    cosq = dt("cosq", [64, OWN], F32, kind="ExternalInput").ap()
    sinq = dt("sinq", [64, OWN], F32, kind="ExternalInput").ap()
    masks = dt("masks", [32 * 128, 128], BF16, kind="ExternalInput").ap()
    wq = dt("wq", [DM, NH * DH], F32, kind="ExternalInput").ap()
    wk = dt("wk", [DM, NH * DH], F32, kind="ExternalInput").ap()
    wv = dt("wv", [DM, NH * DH], F32, kind="ExternalInput").ap()
    wo = dt("wo", [NH, DH, DM], F32, kind="ExternalInput").ap()
    gate_wT = dt("gate_wT", [DM, NE], F32, kind="ExternalInput").ap()
    gu_wT = dt("gu_wT", [NE, DM, I2], F32, kind="ExternalInput").ap()
    dp_wT = dt("dp_wT", [NE, II, DM], F32, kind="ExternalInput").ap()
    shgT = dt("shgT", [DM, SI], F32, kind="ExternalInput").ap()
    shuT = dt("shuT", [DM, SI], F32, kind="ExternalInput").ap()
    shdT = dt("shdT", [SI, DM], F32, kind="ExternalInput").ap()
    g1h = dt("g1h", [128, 16], F32, kind="ExternalInput").ap()
    c_ident = dt("c_ident", [128, 128], BF16, kind="ExternalInput").ap()
    c_ones_row_bf = dt("c_ones_row_bf", [1, 128], BF16, kind="ExternalInput").ap()
    c_ones_row_f32 = dt("c_ones_row_f32", [1, 128], F32, kind="ExternalInput").ap()
    c_ones_col_bf = dt("c_ones_col_bf", [128, 1], BF16, kind="ExternalInput").ap()

    out = dt("out", [OWN, DM], F32, kind="ExternalOutput").ap()

    Kscr = dt("Kscr", [NH, DH, T], BF16, kind="Internal").ap()
    Vscr = dt("Vscr", [NH, T, DH], BF16, kind="Internal").ap()
    XLscr = dt("XLscr", [DM, T], BF16, kind="Internal").ap()
    XL2scr = dt("XL2scr", [DM, OWN], BF16, kind="Internal").ap()
    X1scr = dt("X1scr", [OWN, DM], F32, kind="Internal").ap()
    WTscr = dt("WTscr", [NE, OWN], BF16, kind="Internal").ap()

    if dbg:
        dbg_xl = dt("dbg_xl", [DM, T], BF16, kind="ExternalOutput").ap()
        dbg_k = dt("dbg_k", [NH, DH, T], BF16, kind="ExternalOutput").ap()
        dbg_v = dt("dbg_v", [NH, T, DH], BF16, kind="ExternalOutput").ap()
        dbg_q = dt("dbg_q", [128, NH * OWN], BF16, kind="ExternalOutput").ap()
        dbg_ot = dt("dbg_ot", [128, NH * OWN], BF16, kind="ExternalOutput").ap()
        dbg_rr = dt("dbg_rr", [NH, OWN], F32, kind="ExternalOutput").ap()
        dbg_br = dt("dbg_br", [NH, OWN], F32, kind="ExternalOutput").ap()
        dbg_sp = dt("dbg_sp", [4, 128, 1024], F32, kind="ExternalOutput").ap()
        dbg_pp = dt("dbg_pp", [4, 128, 1024], BF16, kind="ExternalOutput").ap()
        dbg_x1s = dt("dbg_x1s", [OWN, DM], F32, kind="ExternalOutput").ap()
        dbg_sg = dt("dbg_sg", [2, 128, 4, 512], F32, kind="ExternalOutput").ap()
        dbg_shint = dt("dbg_shint", [2, 128, 4, 512], BF16, kind="ExternalOutput").ap()
        dbg_shg = dt("dbg_shg", [2, 128, 4], F32, kind="ExternalOutput").ap()
        dbg_shgb = dt("dbg_shgb", [128, 16, 512], BF16, kind="ExternalOutput").ap()
        dbg_x1 = dt("dbg_x1", [OWN, DM], F32, kind="ExternalOutput").ap()
        dbg_xl2 = dt("dbg_xl2", [DM, OWN], BF16, kind="ExternalOutput").ap()
        dbg_wt = dt("dbg_wt", [NE, OWN], BF16, kind="ExternalOutput").ap()

    xfr = x_feat.rearrange("(c p) t -> p c t", p=128)
    xofr = x_own_feat.rearrange("(c p) t -> p c t", p=128)
    xlr = XLscr.rearrange("(c p) t -> p c t", p=128)
    xl2r = XL2scr.rearrange("(c p) t -> p c t", p=128)
    wqr = wq.rearrange("(c p) n -> p c n", p=128)
    wkr = wk.rearrange("(c p) n -> p c n", p=128)
    wvr = wv.rearrange("(c p) n -> p c n", p=128)
    gwr = gate_wT.rearrange("(c p) n -> p c n", p=128)
    shgr = shgT.rearrange("(c p) n -> p c n", p=128)
    shur = shuT.rearrange("(c p) n -> p c n", p=128)
    shdr = shdT.rearrange("(c p) n -> p c n", p=128)
    maskr = masks.rearrange("(c p) q -> p c q", p=128)

    P = Plan()

    gstack = contextlib.ExitStack()
    qt_stack = contextlib.ExitStack()
    qt = qt_stack.enter_context(nc.sbuf_tensor("qt", [128, NH, OWN], BF16, side="right"))
    ident = gstack.enter_context(nc.sbuf_tensor("ident", [128, 128], BF16))
    ones_row_bf = gstack.enter_context(nc.sbuf_tensor("ones_row_bf", [1, 128], BF16))
    ones_row_f32 = gstack.enter_context(nc.sbuf_tensor("ones_row_f32", [1, 128], F32))
    ones_col_bf = gstack.enter_context(nc.sbuf_tensor("ones_col_bf", [128, 1], BF16))

    tok_consts = P.step("sync", lambda e: [
        e.dma_start(ident[:], c_ident),
        e.dma_start(ones_row_bf[:], c_ones_row_bf),
        e.dma_start(ones_row_f32[:], c_ones_row_f32),
        e.dma_start(ones_col_bf[:], c_ones_col_bf),
    ], n_dma=4)

    P.phase = "phA"
    xlo_stack = contextlib.ExitStack()
    xlo = xlo_stack.enter_context(nc.sbuf_tensor("xlo", [128, NH, OWN], BF16, side="right"))

    with contextlib.ExitStack() as sa:
        xst = [sa.enter_context(nc.sbuf_tensor(f"xst{i}", [128, 16, SBK], F32)) for i in range(2)]
        xbf = sa.enter_context(nc.sbuf_tensor("xbf", [128, 16, SBK], BF16))
        xsq = sa.enter_context(nc.sbuf_tensor("xsq", [128, 16, SBK], BF16))
        xln = [sa.enter_context(nc.sbuf_tensor(f"xln{i}", [128, 16, SBK], BF16)) for i in range(2)]
        ta = sa.enter_context(nc.sbuf_tensor("ta", [128, SBK], F32))
        tb_ = sa.enter_context(nc.sbuf_tensor("tb_", [128, SBK], F32))
        rstd_sb = sa.enter_context(nc.sbuf_tensor("rstd_sb", [128, SBK], F32))
        mrs_sb = sa.enter_context(nc.sbuf_tensor("mrs_sb", [128, SBK], F32))
        cext = [sa.enter_context(nc.sbuf_tensor(f"cext{i}", [128, SBK], F32)) for i in range(2)]
        sext = [sa.enter_context(nc.sbuf_tensor(f"sext{i}", [128, SBK], F32)) for i in range(2)]
        kqo = [sa.enter_context(nc.sbuf_tensor(f"kqo{i}", [128, SBK], BF16)) for i in range(3)]
        vev = [sa.enter_context(nc.sbuf_tensor(f"vev{i}", [128, 512], BF16)) for i in range(2)]
        rows = sa.enter_context(nc.sbuf_tensor("rows", [1, 3, SBK], F32))

        with contextlib.ExitStack() as spa:
            ps_s = spa.enter_context(nc.psum_tensor("ps_s", [1, SBK], F32))
            ps_s2 = spa.enter_context(nc.psum_tensor("ps_s2", [1, SBK], F32))
            ps_bc = spa.enter_context(nc.psum_tensor("ps_bc", [128, 2 * SBK], F32))
            psK = [spa.enter_context(nc.psum_tensor(f"psK{i}", [128, SBK], F32)) for i in range(3)]
            psV = [spa.enter_context(nc.psum_tensor(f"psV{i}", [128, 512], F32)) for i in range(2)]

            with contextlib.ExitStack() as swa:
                wkb = [swa.enter_context(nc.sbuf_tensor(f"wkb{i}", [128, 16, 512], BF16))
                       for i in range(2)]
                wvb = [swa.enter_context(nc.sbuf_tensor(f"wvb{i}", [128, 16, 512], BF16))
                       for i in range(2)]

                st = {
                    "x_used": [tok_consts, tok_consts],
                    "xbf_used": None,
                    "xln_used": [[], []],
                    "rope_last": [None, None],
                    "kqo_used": [None, None, None],
                    "vev_used": [None, None],
                    "prev_trow": None,
                    "prev_sq": None,
                    "prev_bc": None,
                    "prev_norm": None,
                    "kidx": 0,
                }

                def ln_block(src3d, blk_off, into, par):
                    tok_dma = P.step("sync", lambda e: e.dma_start(
                        xst[par][:], src3d[:, :, blk_off:blk_off + SBK]),
                        waits=[st["x_used"][par]], n_dma=1)
                    w = [tok_dma]
                    if st["xbf_used"]:
                        w.append(st["xbf_used"])
                    tcast = P.step("vector", lambda e, par=par: [
                        e.tensor_copy(xbf[:, kc, :], xst[par][:, kc, :]) for kc in range(16)
                    ], waits=w)
                    w = [tcast]
                    if st["prev_sq"]:
                        w.append(st["prev_sq"])
                    tsq = P.step("scalar", lambda e: [
                        e.activation(xsq[:, kc, :], xbf[:, kc, :], AF.Square)
                        for kc in range(16)
                    ], waits=w)
                    w = [tcast, tok_consts]
                    if st["prev_trow"]:
                        w.append(st["prev_trow"])
                    tmm1 = P.step("tensor", lambda e: [
                        e.matmul(ps_s[:], ones_col_bf[:], xbf[:, kc, :],
                                 start=(kc == 0), stop=(kc == 15), skip_group_check=True)
                        for kc in range(16)
                    ], waits=w)
                    st["xbf_used"] = tmm1
                    w = [tsq]
                    if st["prev_trow"]:
                        w.append(st["prev_trow"])
                    tmm2 = P.step("tensor", lambda e: [
                        e.matmul(ps_s2[:], ones_col_bf[:], xsq[:, kc, :],
                                 start=(kc == 0), stop=(kc == 15), skip_group_check=True)
                        for kc in range(16)
                    ], waits=w)
                    st["prev_sq"] = tmm2
                    trow = P.step("vector", lambda e: [
                        e.tensor_scalar(rows[0:1, 0, :], ps_s[:], 1.0 / DM, None, ALU.mult),
                        e.tensor_scalar(rows[0:1, 1, :], ps_s2[:], 1.0 / DM, None, ALU.mult),
                        e.tensor_tensor(rows[0:1, 2, :], rows[0:1, 0, :], rows[0:1, 0, :], ALU.mult),
                    ], waits=[tmm1, tmm2])
                    st["prev_trow"] = trow
                    trow2 = P.step("vector", lambda e: e.scalar_tensor_tensor(
                        rows[0:1, 2, :], rows[0:1, 1, :], 1e-5, rows[0:1, 2, :],
                        ALU.add, ALU.subtract), waits=[trow])
                    tr1 = P.step("scalar", lambda e: e.activation(
                        rows[0:1, 1, :], rows[0:1, 2, :], AF.Ln), waits=[trow2])
                    tr2 = P.step("scalar", lambda e: e.activation(
                        rows[0:1, 1, :], rows[0:1, 1, :], AF.Exp, scale=-0.5), waits=[tr1])
                    tr3 = P.step("vector", lambda e: e.tensor_tensor(
                        rows[0:1, 0, :], rows[0:1, 0, :], rows[0:1, 1, :], ALU.mult), waits=[tr2])
                    w = [tr2, tr3]
                    if st["prev_bc"]:
                        w.append(st["prev_bc"])
                    tbc = P.step("tensor", lambda e: [
                        e.matmul(ps_bc[:, 0:SBK], ones_row_f32[:], rows[0:1, 1, :],
                                 start=True, stop=True, skip_group_check=True),
                        e.matmul(ps_bc[:, SBK:2 * SBK], ones_row_f32[:], rows[0:1, 0, :],
                                 start=True, stop=True, skip_group_check=True),
                    ], waits=w)
                    w = [tbc]
                    if st["prev_norm"]:
                        w.append(st["prev_norm"])
                    tevc = P.step("vector", lambda e: [
                        e.tensor_copy(rstd_sb[:], ps_bc[:, 0:SBK]),
                        e.tensor_copy(mrs_sb[:], ps_bc[:, SBK:2 * SBK]),
                    ], waits=w)
                    st["prev_bc"] = tevc
                    ops = []
                    w = [tevc] + st["xln_used"][into]
                    def norm_fn(e, par=par, into=into):
                        out = []
                        for kc in range(16):
                            out.append(e.tensor_tensor(ta[:, 0:SBK], xst[par][:, kc, :],
                                                       rstd_sb[:], ALU.mult))
                            out.append(e.tensor_tensor(xln[into][:, kc, :], ta[:, 0:SBK],
                                                       mrs_sb[:], ALU.subtract))
                        return out
                    tnorm = P.step("vector", norm_fn, waits=w)
                    st["xln_used"][into] = []
                    st["x_used"][par] = tnorm
                    st["prev_norm"] = tnorm
                    return tnorm

                xlo_done = []
                for blk in range(OWN // SBK):
                    par = blk % 2
                    tnorm = ln_block(xofr, blk * SBK, 0, par)
                    tcp = P.step("vector", lambda e, blk=blk: [
                        e.tensor_copy(xlo[:, kc, blk * SBK:(blk + 1) * SBK], xln[0][:, kc, :])
                        for kc in range(16)
                    ], waits=[tnorm])
                    st["xln_used"][0] = [tcp]
                    xlo_done = [tcp]

                def prep(sb, half):
                    into = par = sb % 2
                    t0 = sb * SBK
                    if half == 0:
                        tnorm = ln_block(xfr, t0, into, par)
                        txw = P.step("sync", lambda e, t0=t0, into=into: e.dma_start(
                            xlr[:, :, t0:t0 + SBK], xln[into][:]), waits=[tnorm], n_dma=1)
                        st["xln_used"][into].append(txw)
                        return tnorm
                    tok = P.step("sync", lambda e, t0=t0, into=into: e.dma_start(
                        xln[into][:], xlr[:, :, t0:t0 + SBK]),
                        waits=st["xln_used"][into], n_dma=1)
                    st["xln_used"][into] = []
                    return tok

                def kv_block(sb, h0, into, xln_ready, twk, ppar):
                    t0 = sb * SBK
                    par = sb % 2
                    wct = [st["rope_last"][par]] if st["rope_last"][par] else []
                    tct = P.step("scalard", lambda e, t0=t0, par=par: [
                        e.dma_start(cext[par][0:64, :], cosk[:, t0:t0 + SBK]),
                        e.dma_start(cext[par][64:128, :], cosk[:, t0:t0 + SBK]),
                        e.dma_start(sext[par][0:64, :], sink[:, t0:t0 + SBK]),
                        e.dma_start(sext[par][64:128, :], sink[:, t0:t0 + SBK]),
                    ], waits=wct, n_dma=4)
                    for hh in range(4):
                        h = h0 + hh
                        ki = st["kidx"] % 3
                        st["kidx"] += 1
                        wmm = [xln_ready, twk]
                        if st["kqo_used"][ki]:
                            wmm.append(st["kqo_used"][ki])
                        tmm = P.step("tensor", lambda e, hh=hh, into=into, ki=ki, ppar=ppar: [
                            e.matmul(psK[ki][:], wkb[ppar][:, kc, hh * DH:(hh + 1) * DH],
                                     xln[into][:, kc, :], start=(kc == 0), stop=(kc == 15),
                                     skip_group_check=True)
                            for kc in range(16)
                        ], waits=wmm)
                        trp = P.step("vector", lambda e, ki=ki, par=par: [
                            e.tensor_tensor(ta[:, 0:SBK], psK[ki][:], cext[par][:], ALU.mult),
                            e.tensor_tensor(tb_[0:64, 0:SBK], psK[ki][64:128, :],
                                            sext[par][64:128, :], ALU.mult),
                            e.tensor_tensor(tb_[64:128, 0:SBK], psK[ki][0:64, :],
                                            sext[par][0:64, :], ALU.mult),
                            e.tensor_tensor(kqo[ki][0:64, :], ta[0:64, 0:SBK],
                                            tb_[0:64, 0:SBK], ALU.subtract),
                            e.tensor_tensor(kqo[ki][64:128, :], ta[64:128, 0:SBK],
                                            tb_[64:128, 0:SBK], ALU.add),
                        ], waits=[tmm, tct])
                        st["rope_last"][par] = trp
                        tkw = P.step("scalard", lambda e, h=h, t0=t0, ki=ki: e.dma_start(
                            Kscr[h, :, t0:t0 + SBK], kqo[ki][:]), waits=[trp], n_dma=1)
                        st["kqo_used"][ki] = tkw
                    for hg in range(1):
                        for tt in range(2):
                            vpar = tt % 2
                            wmm = [xln_ready, twk]
                            if st["vev_used"][vpar]:
                                wmm.append(st["vev_used"][vpar])
                            tmm = P.step("tensor", lambda e, tt=tt, into=into, vpar=vpar, ppar=ppar: [
                                e.matmul(psV[vpar][:], xln[into][:, kc, tt * 128:(tt + 1) * 128],
                                         wvb[ppar][:, kc, :],
                                         start=(kc == 0), stop=(kc == 15), skip_group_check=True)
                                for kc in range(16)
                            ], waits=wmm)
                            tev = P.step("scalar", lambda e, vpar=vpar: e.activation(
                                vev[vpar][:], psV[vpar][:], AF.Copy), waits=[tmm])
                            tvw = P.step("scalard", lambda e, h0=h0, tt=tt, t0=t0, vpar=vpar: [
                                e.dma_start(
                                    Vscr[h0 + hh2,
                                         t0 + tt * 128:t0 + tt * 128 + 128, :],
                                    vev[vpar][:, hh2 * 128:(hh2 + 1) * 128])
                                for hh2 in range(4)
                            ], waits=[tev], n_dma=4)
                            st["vev_used"][vpar] = tvw
                    st["xln_used"][into].append(P.tok("tensor"))

                wpass_used = [None, None]
                twk_all = {}

                def load_pass_weights(q):
                    ppar = q % 2
                    h0 = q * 4
                    w = [t for t in [wpass_used[ppar]] if t]
                    twk_all[q] = P.step("gpsimd", lambda e, h0=h0, ppar=ppar: [
                        e.dma_start(wkb[ppar][:], wkr[:, :, h0 * DH:(h0 + 4) * DH]),
                        e.dma_start(wvb[ppar][:], wvr[:, :, h0 * DH:(h0 + 4) * DH]),
                    ], waits=w, n_dma=2)

                load_pass_weights(0)
                load_pass_weights(1)
                for qtr in range(4):
                    h0 = qtr * 4
                    ppar = qtr % 2
                    ready = {0: prep(0, qtr)}
                    for sb in range(NSB):
                        if sb + 1 < NSB:
                            ready[sb + 1] = prep(sb + 1, qtr)
                        kv_block(sb, h0, sb % 2, ready[sb], twk_all[qtr], ppar)
                    wpass_used[ppar] = P.tok("tensor")
                    if qtr + 2 < 4:
                        load_pass_weights(qtr + 2)

        P.phase = "phQ"
        with contextlib.ExitStack() as sq_:
            psQ = [sq_.enter_context(nc.psum_tensor(f"psQ{i}", [128, 512], F32)) for i in range(2)]
            wqb = sq_.enter_context(nc.sbuf_tensor("wqb", [128, 16, 1024], BF16))
            cqe = sq_.enter_context(nc.sbuf_tensor("cqe", [128, OWN], F32))
            sqe = sq_.enter_context(nc.sbuf_tensor("sqe", [128, OWN], F32))
            qta = sq_.enter_context(nc.sbuf_tensor("qta", [128, 512], F32))
            qtb = sq_.enter_context(nc.sbuf_tensor("qtb", [128, 512], F32))
            tcq = P.step("sync", lambda e: [
                e.dma_start(cqe[0:64, :], cosq),
                e.dma_start(cqe[64:128, :], cosq),
                e.dma_start(sqe[0:64, :], sinq),
                e.dma_start(sqe[64:128, :], sinq),
            ], n_dma=4)
            rope_done = [None, None]
            for half in range(2):
                h0 = half * 8
                twq = P.step("gpsimd", lambda e, h0=h0: e.dma_start(
                    wqb[:], wqr[:, :, h0 * DH:(h0 + 8) * DH]),
                    waits=[P.tok("tensor")], n_dma=1)
                for hh in range(8):
                    h = h0 + hh
                    for qb in range(2):
                        qpar = (hh * 2 + qb) % 2
                        w = [twq, tcq] + xlo_done
                        if rope_done[qpar]:
                            w.append(rope_done[qpar])
                        tmm = P.step("tensor", lambda e, hh=hh, qb=qb, qpar=qpar: [
                            e.matmul(psQ[qpar][:], wqb[:, kc, hh * DH:(hh + 1) * DH],
                                     xlo[:, kc, qb * 512:(qb + 1) * 512],
                                     start=(kc == 0), stop=(kc == 15), skip_group_check=True)
                            for kc in range(16)
                        ], waits=w)
                        trp = P.step("vector", lambda e, h=h, qb=qb, qpar=qpar: [
                            e.tensor_tensor(qta[:], psQ[qpar][:], cqe[:, qb * 512:(qb + 1) * 512], ALU.mult),
                            e.tensor_tensor(qtb[0:64, :], psQ[qpar][64:128, :],
                                            sqe[64:128, qb * 512:(qb + 1) * 512], ALU.mult),
                            e.tensor_tensor(qtb[64:128, :], psQ[qpar][0:64, :],
                                            sqe[0:64, qb * 512:(qb + 1) * 512], ALU.mult),
                            e.tensor_tensor(qt[0:64, h, qb * 512:(qb + 1) * 512], qta[0:64, :],
                                            qtb[0:64, :], ALU.subtract),
                            e.tensor_tensor(qt[64:128, h, qb * 512:(qb + 1) * 512], qta[64:128, :],
                                            qtb[64:128, :], ALU.add),
                        ], waits=[tmm])
                        rope_done[qpar] = trp

    xlo_stack.close()
    P.barrier()

    if dbg:
        P.step("sync", lambda e: [
            e.dma_start(dbg_xl, XLscr),
            e.dma_start(dbg_k, Kscr),
            e.dma_start(dbg_v, Vscr),
            e.dma_start(dbg_q.rearrange("p (h q) -> p h q", h=NH), qt[:]),
        ], n_dma=4)
        P.barrier()

    P.phase = "phB"
    ot_stack = contextlib.ExitStack()
    ot = ot_stack.enter_context(nc.sbuf_tensor("ot", [128, NH, OWN], BF16, side="right"))

    with contextlib.ExitStack() as sb_:
        masks_sb = sb_.enter_context(nc.sbuf_tensor("masks_sb", [128, 32, 128], BF16))
        kbuf = sb_.enter_context(nc.sbuf_tensor("kbuf", [128, 2, T], BF16))
        vbuf = sb_.enter_context(nc.sbuf_tensor("vbuf", [128, 2, 32, DH], BF16))
        enbp = [sb_.enter_context(nc.sbuf_tensor(f"enbp{i}", [128, 1024], BF16)) for i in range(2)]
        bmm01 = [sb_.enter_context(nc.sbuf_tensor(f"bmm01{i}", [128, 32, 128], BF16)) for i in range(2)]
        pexp = [sb_.enter_context(nc.sbuf_tensor(f"pe{i}", [128, 1024], BF16)) for i in range(2)]
        pbuf = [sb_.enter_context(nc.sbuf_tensor(f"pb{i}", [128, 1024], BF16)) for i in range(2)]
        k2 = sb_.enter_context(nc.sbuf_tensor("k2", [128, T], BF16))
        q2 = sb_.enter_context(nc.sbuf_tensor("q2", [128, OWN], BF16))
        rkm8 = sb_.enter_context(nc.sbuf_tensor("rkm8", [1, 8], F32))
        rkmax = sb_.enter_context(nc.sbuf_tensor("rkmax", [1, 1], F32))
        rq2 = sb_.enter_context(nc.sbuf_tensor("rq2", [1, 1024], F32))
        brow = sb_.enter_context(nc.sbuf_tensor("brow", [1, 1024], F32))
        brow_bf = sb_.enter_context(nc.sbuf_tensor("brow_bf", [1, 1024], BF16))
        rrow = sb_.enter_context(nc.sbuf_tensor("rrow", [1, 1024], F32))
        rbc = sb_.enter_context(nc.sbuf_tensor("rbc", [128, 1024], F32))

        with contextlib.ExitStack() as spb:
            psS = [spb.enter_context(nc.psum_tensor(f"psS{i}", [128, 1024], F32)) for i in range(2)]
            psO = spb.enter_context(nc.psum_tensor("psO", [128, 1024], F32))
            psSum = spb.enter_context(nc.psum_tensor("psSum", [1, 1024], F32))

            tmask = P.step("sync", lambda e: e.dma_start(masks_sb[:], maskr), n_dma=1)
            kv_loaded = [None, None]
            kv_freed = [None, None]
            prev_tOev = None
            prev_trc = None
            prev_enb_used = [None, None]

            kv_loaded[0] = P.step("sync", lambda e: [
                e.dma_start(kbuf[:, 0, :], Kscr[0]),
                e.dma_start(vbuf[:, 0, :, :], Vscr[0].rearrange("(c p) d -> p c d", p=128)),
            ], n_dma=2)

            for h in range(NH):
                par = h % 2
                if h + 1 < NH:
                    w = [kv_freed[1 - par]] if kv_freed[1 - par] else []
                    kv_loaded[1 - par] = P.step("sync", lambda e, h=h, par=par: [
                        e.dma_start(kbuf[:, 1 - par, :], Kscr[h + 1]),
                        e.dma_start(vbuf[:, 1 - par, :, :],
                                    Vscr[h + 1].rearrange("(c p) d -> p c d", p=128)),
                    ], waits=w, n_dma=2)

                tk2 = P.step("scalar", lambda e, par=par: [
                    e.activation(k2[:, i * 1024:(i + 1) * 1024],
                                 kbuf[:, par, i * 1024:(i + 1) * 1024], AF.Square)
                    for i in range(4)
                ], waits=[kv_loaded[par]])
                tq2 = P.step("scalar", lambda e, h=h: e.activation(q2[:], qt[:, h, :], AF.Square))
                prev_ev = prev_trc
                rk_toks = []
                for ch in range(8):
                    seg = ch % 2
                    w = [tk2, tok_consts] + ([prev_ev] if prev_ev else [])
                    tmmr = P.step("tensor", lambda e, ch=ch, seg=seg: e.matmul(
                        psSum[0:1, seg * 512:(seg + 1) * 512], ones_col_bf[:],
                        k2[:, ch * 512:(ch + 1) * 512], start=True, stop=True,
                        skip_group_check=True), waits=w)
                    tev = P.step("vector", lambda e, ch=ch, seg=seg: e.reduce_max(
                        rkm8[0:1, ch:ch + 1], psSum[0:1, seg * 512:(seg + 1) * 512], axis=AX.X),
                        waits=[tmmr])
                    prev_ev = tev
                    rk_toks.append(tev)
                tqr = []
                for seg in range(2):
                    tmmq = P.step("tensor", lambda e, seg=seg: e.matmul(
                        psSum[0:1, seg * 512:(seg + 1) * 512], ones_col_bf[:],
                        q2[:, seg * 512:(seg + 1) * 512], start=True, stop=True,
                        skip_group_check=True), waits=[tq2, rk_toks[-1]])
                    tevq = P.step("vector", lambda e, seg=seg: e.tensor_copy(
                        rq2[0:1, seg * 512:(seg + 1) * 512],
                        psSum[0:1, seg * 512:(seg + 1) * 512]), waits=[tmmq])
                    tqr.append(tevq)
                tmx = P.step("vector", lambda e: e.reduce_max(rkmax[:], rkm8[:], axis=AX.X),
                             waits=rk_toks)
                tbr = P.step("vector", lambda e: e.tensor_scalar(
                    brow[:], rq2[:], rkmax[:], 0.5, ALU.add, ALU.mult), waits=[tmx] + tqr)
                tbr2 = P.step("vector", lambda e: e.tensor_copy(brow_bf[:], brow[:]), waits=[tbr])
                w = [tbr2] + ([prev_tOev] if prev_tOev else [])
                tbc = P.step("tensor", lambda e: [
                    e.matmul(psO[:, s * 512:(s + 1) * 512], ones_row_bf[:],
                             brow_bf[0:1, s * 512:(s + 1) * 512], start=True, stop=True,
                             skip_group_check=True)
                    for s in range(2)
                ], waits=w)
                hpar = h % 2
                w = [tbc]
                if prev_enb_used[hpar]:
                    w.append(prev_enb_used[hpar])
                tenb = P.step("scalar", lambda e, hpar=hpar: e.activation(
                    enbp[hpar][:], psO[:], AF.Exp, scale=-1.0), waits=w)
                tprep = P.step("vector", lambda e, hpar=hpar: [
                    e.tensor_tensor(bmm01[hpar][:, jc, :],
                                    enbp[hpar][:, (jc // 4) * 128:(jc // 4) * 128 + 128],
                                    masks_sb[:, jc, :], ALU.mult) for jc in range(32)
                ], waits=[tenb, tmask])

                exp_toks = [prev_tOev, prev_tOev]
                mul_toks = [None, None]
                omm_toks = [None, None]
                s_tok = {}

                def s_step(jc):
                    spar = jc % 2
                    qoff = 128 * (jc // 4)
                    segs = segs_for(qoff)
                    w = [kv_loaded[par]]
                    if exp_toks[spar]:
                        w.append(exp_toks[spar])
                    s_tok[jc] = P.step("tensor", lambda e, jc=jc, spar=spar, segs=segs, par=par, h=h: [
                        e.matmul(psS[spar][:, a:b], kbuf[:, par, jc * 128:(jc + 1) * 128],
                                 qt[:, h, a:b], start=True, stop=True, skip_group_check=True)
                        for (a, b) in segs
                    ], waits=w)

                s_step(0)
                for jc in range(32):
                    if serial_attn:
                        P.barrier()
                    g = jc // 4
                    qoff = 128 * g
                    spar = jc % 2
                    segs = segs_for(qoff)
                    w = [s_tok[jc]]
                    if mul_toks[spar]:
                        w.append(mul_toks[spar])
                    texp = P.step("scalar", lambda e, qoff=qoff, spar=spar: e.activation(
                        pexp[spar][:, qoff:1024], psS[spar][:, qoff:1024], AF.Exp), waits=w)
                    exp_toks[spar] = texp
                    w = [texp, tprep]
                    if omm_toks[spar]:
                        w.append(omm_toks[spar])
                    tmul = P.step("vector", lambda e, jc=jc, qoff=qoff, spar=spar, hpar=hpar: (
                        [e.tensor_tensor(pbuf[spar][:, qoff:qoff + 128],
                                         pexp[spar][:, qoff:qoff + 128],
                                         bmm01[hpar][:, jc, :], ALU.mult)] +
                        ([e.tensor_tensor(pbuf[spar][:, qoff + 128:1024],
                                          pexp[spar][:, qoff + 128:1024],
                                          enbp[hpar][:, qoff + 128:1024], ALU.mult)]
                         if qoff + 128 < 1024 else [])
                    ), waits=w)
                    mul_toks[spar] = tmul
                    if serial_attn:
                        P.barrier()
                    if jc + 1 < 32:
                        s_step(jc + 1)
                    if serial_attn:
                        P.barrier()
                    tO = P.step("tensor", lambda e, jc=jc, spar=spar, segs=segs, par=par: [
                        e.matmul(psO[:, a:b], vbuf[:, par, jc, :], pbuf[spar][:, a:b],
                                 start=(jc == 0), stop=(jc == 31), skip_group_check=True)
                        for (a, b) in segs
                    ], waits=[tmul, tenb])
                    w = [tmul]
                    if jc == 0:
                        w += tqr
                    tSum = P.step("tensor", lambda e, jc=jc, spar=spar, segs=segs: [
                        e.matmul(psSum[0:1, a:b], ones_col_bf[:], pbuf[spar][:, a:b],
                                 start=(jc == 0), stop=(jc == 31), skip_group_check=True)
                        for (a, b) in segs
                    ], waits=w)
                    omm_toks[spar] = tSum
                kv_freed[par] = P.tok("tensor")
                prev_enb_used[hpar] = mul_toks[1]

                trc = P.step("vector", lambda e: e.reciprocal(rrow[:], psSum[:]),
                             waits=[omm_toks[0], omm_toks[1]])
                prev_trc = trc
                if dbg:
                    prev_trc = P.step("sync", lambda e, h=h: [
                        e.dma_start(dbg_rr[h][None, :], rrow[:]),
                        e.dma_start(dbg_br[h][None, :], brow[:]),
                    ], waits=[trc, tbr], n_dma=2)
                tbc2 = P.step("tensor", lambda e: [
                    e.matmul(psS[0][:, s * 512:(s + 1) * 512], ones_row_f32[:],
                             rrow[0:1, s * 512:(s + 1) * 512], start=True, stop=True,
                             skip_group_check=True)
                    for s in range(2)
                ], waits=[trc, exp_toks[0], exp_toks[1]])
                trbc = P.step("vector", lambda e: e.tensor_copy(rbc[:], psS[0][:]), waits=[tbc2])
                tOev = P.step("vector", lambda e, h=h: e.tensor_tensor(
                    ot[:, h, :], psO[:], rbc[:], ALU.mult), waits=[trbc, P.tok("tensor")])
                prev_tOev = tOev

    P.barrier()

    if dbg:
        P.step("sync", lambda e: e.dma_start(
            dbg_ot.rearrange("p (h q) -> p h q", h=NH), ot[:]), n_dma=1)
        P.barrier()

    P.phase = "phOUT"
    with contextlib.ExitStack() as so:
        wob = so.enter_context(nc.sbuf_tensor("wob", [128, NH, DM], BF16))
        gwb = so.enter_context(nc.sbuf_tensor("gwb", [128, 16, NE], BF16))
        xot = [so.enter_context(nc.sbuf_tensor(f"xot{i}", [128, DM], F32)) for i in range(2)]
        x1t = [so.enter_context(nc.sbuf_tensor(f"x1t{i}", [128, DM], F32)) for i in range(2)]
        sqt = so.enter_context(nc.sbuf_tensor("sqt", [128, DM], F32))
        st1 = so.enter_context(nc.sbuf_tensor("st1", [128, 8], F32))
        xl2t = so.enter_context(nc.sbuf_tensor("xl2t", [128, DM], BF16))
        xl2tt = so.enter_context(nc.sbuf_tensor("xl2tt", [128, 16, 128], BF16))
        lg = so.enter_context(nc.sbuf_tensor("lg", [128, NE], F32))
        m8 = so.enter_context(nc.sbuf_tensor("m8", [128, 8], F32))
        nm1 = so.enter_context(nc.sbuf_tensor("nm1", [128, 1], F32))
        e8 = so.enter_context(nc.sbuf_tensor("e8", [128, 8], F32))
        eL = so.enter_context(nc.sbuf_tensor("eL", [128, NE], F32))
        selb = so.enter_context(nc.sbuf_tensor("selb", [128, NE], F32))
        s8 = so.enter_context(nc.sbuf_tensor("s8", [128, 1], F32))
        r8 = so.enter_context(nc.sbuf_tensor("r8", [128, 1], F32))
        wnb = so.enter_context(nc.sbuf_tensor("wnb", [128, NE], BF16))
        wtsb = so.enter_context(nc.sbuf_tensor("wtsb", [64, OWN], BF16))

        with contextlib.ExitStack() as spo:
            psR = [spo.enter_context(nc.psum_tensor(f"psR{i}", [128, 512], F32)) for i in range(2)]
            psL = spo.enter_context(nc.psum_tensor("psL", [128, NE], F32))
            psWT = spo.enter_context(nc.psum_tensor("psWT", [64, 128], BF16))
            psTr = [spo.enter_context(nc.psum_tensor(f"psTr{i}", [128, 128], BF16)) for i in range(2)]

            two = P.step("gpsimd", lambda e: [
                e.dma_start(wob[:, hh, :], wo[hh]) for hh in range(NH)
            ], n_dma=NH)
            tgw = P.step("gpsimd", lambda e: e.dma_start(gwb[:], gwr), n_dma=1)

            xo_tok = [None, None]
            xo_used = [None, None]
            prev_ev = [None, None]
            prev_trev = [None, None]
            prev_wtev = None
            prev_lg = None
            prev_teL = None
            prev_x1_readers = [[], []]
            ev_done = {}
            prev_tst = [None]
            prev_txl2w = [None]
            prev_ttw = [None]

            def stage1(tt):
                par = tt % 2
                w = [xo_used[par]] if xo_used[par] else []
                xo_tok[par] = P.step("sync", lambda e, tt=tt, par=par: e.dma_start(
                    xot[par][:], x_own[tt * 128:(tt + 1) * 128, :]), waits=w, n_dma=1)
                ev_toks = []
                for mb in range(4):
                    rpar = mb % 2
                    w = [two]
                    if prev_ev[rpar]:
                        w.append(prev_ev[rpar])
                    tmm = P.step("tensor", lambda e, tt=tt, mb=mb, rpar=rpar: [
                        e.matmul(psR[rpar][:], ot[:, hh, tt * 128:(tt + 1) * 128],
                                 wob[:, hh, mb * 512:(mb + 1) * 512],
                                 start=(hh == 0), stop=(hh == NH - 1), skip_group_check=True)
                        for hh in range(NH)
                    ], waits=w)
                    tev = P.step("vector", lambda e, mb=mb, rpar=rpar, par=par: e.tensor_tensor(
                        x1t[par][:, mb * 512:(mb + 1) * 512], psR[rpar][:],
                        xot[par][:, mb * 512:(mb + 1) * 512], ALU.add),
                        waits=[tmm, xo_tok[par]] + (prev_x1_readers[par] if mb == 0 else []))
                    prev_ev[rpar] = tev
                    ev_toks.append(tev)
                xo_used[par] = ev_toks[-1]
                ev_done[tt] = ev_toks

            def stage2(tt):
                nonlocal prev_wtev, prev_lg, prev_teL
                par = tt % 2
                ev_toks = ev_done[tt]
                tx1w = P.step("sync", lambda e, tt=tt, par=par: e.dma_start(
                    X1scr[tt * 128:(tt + 1) * 128, :], x1t[par][:]), waits=ev_toks, n_dma=1)
                w = list(ev_toks)
                if prev_tst[0]:
                    w.append(prev_tst[0])
                tsq = P.step("scalar", lambda e, par=par: e.activation(
                    sqt[:], x1t[par][:], AF.Square), waits=w)
                tst = P.step("vector", lambda e, par=par: [
                    e.reduce_sum(st1[:, 0:1], x1t[par][:], axis=AX.X),
                    e.reduce_sum(st1[:, 1:2], sqt[:], axis=AX.X),
                ], waits=[tsq] + ev_toks)
                prev_tst[0] = tst
                tst2 = P.step("vector", lambda e: [
                    e.tensor_scalar(st1[:, 2:3], st1[:, 0:1], 1.0 / DM, None, ALU.mult),
                    e.tensor_scalar(st1[:, 3:4], st1[:, 1:2], 1.0 / DM, None, ALU.mult),
                ], waits=[tst])
                tst3 = P.step("vector", lambda e: [
                    e.tensor_tensor(st1[:, 4:5], st1[:, 2:3], st1[:, 2:3], ALU.mult),
                ], waits=[tst2])
                tst4 = P.step("vector", lambda e: [
                    e.scalar_tensor_tensor(st1[:, 5:6], st1[:, 3:4], 1e-5, st1[:, 4:5],
                                           ALU.add, ALU.subtract),
                ], waits=[tst3])
                tr1 = P.step("scalar", lambda e: e.activation(
                    st1[:, 6:7], st1[:, 5:6], AF.Ln), waits=[tst4])
                tr2 = P.step("scalar", lambda e: e.activation(
                    st1[:, 6:7], st1[:, 6:7], AF.Exp, scale=-0.5), waits=[tr1])
                w = [tr2, tst2, tx1w]
                if prev_trev[0]:
                    w += [prev_trev[0], prev_trev[1]]
                tnorm = P.step("vector", lambda e, par=par: e.tensor_scalar(
                    xl2t[:], x1t[par][:], st1[:, 2:3], st1[:, 6:7], ALU.subtract, ALU.mult),
                    waits=w)
                prev_x1_readers[par] = [tnorm, tx1w]
                tr_toks = []
                for mc in range(16):
                    tpar = mc % 2
                    w = [tnorm, tok_consts]
                    if prev_txl2w[0]:
                        w.append(prev_txl2w[0])
                    if prev_trev[tpar]:
                        w.append(prev_trev[tpar])
                    ttr = P.step("tensor", lambda e, mc=mc, tpar=tpar: e.transpose(
                        psTr[tpar][:], xl2t[:, mc * 128:(mc + 1) * 128], ident[:]), waits=w)
                    tte = P.step("vector", lambda e, mc=mc, tpar=tpar: e.tensor_copy(
                        xl2tt[:, mc, :], psTr[tpar][:]), waits=[ttr])
                    prev_trev[tpar] = tte
                    tr_toks.append(tte)
                txl2w = P.step("sync", lambda e, tt=tt: e.dma_start(
                    xl2r[:, :, tt * 128:(tt + 1) * 128], xl2tt[:]), waits=tr_toks, n_dma=1)
                prev_txl2w[0] = txl2w
                w = [tgw] + tr_toks
                if prev_lg:
                    w.append(prev_lg)
                tlg = P.step("tensor", lambda e: [
                    e.matmul(psL[:], xl2tt[:, kc, :], gwb[:, kc, :],
                             start=(kc == 0), stop=(kc == 15), skip_group_check=True)
                    for kc in range(16)
                ], waits=w)
                tlgc = P.step("vector", lambda e: e.tensor_copy(lg[:], psL[:]), waits=[tlg])
                prev_lg = tlgc
                tm8 = P.step("vector", lambda e: e.max(m8[:], lg[:]), waits=[tlgc])
                w = [tm8]
                if prev_teL:
                    w.append(prev_teL)
                tnm = P.step("vector", lambda e: e.tensor_scalar(
                    nm1[:], m8[:, 0:1], -1.0, None, ALU.mult), waits=w)
                te8 = P.step("scalar", lambda e: e.activation(e8[:], m8[:], AF.Exp, bias=nm1[:]),
                             waits=[tnm])
                teL = P.step("scalar", lambda e: e.activation(eL[:], lg[:], AF.Exp, bias=nm1[:]),
                             waits=[tnm, tlgc])
                prev_teL = teL
                ts8 = P.step("vector", lambda e: e.reduce_sum(s8[:], e8[:], axis=AX.X), waits=[te8])
                tr8 = P.step("vector", lambda e: e.reciprocal(r8[:], s8[:]), waits=[ts8])
                tsel = P.step("vector", lambda e: e.tensor_single_scalar(
                    selb[:], lg[:], m8[:, 7:8], ALU.is_ge), waits=[tm8])
                tw_ = P.step("vector", lambda e: e.tensor_tensor(
                    selb[:], eL[:], selb[:], ALU.mult), waits=[tsel, teL])
                w = [tw_, tr8]
                if prev_ttw[0]:
                    w.append(prev_ttw[0])
                twn = P.step("vector", lambda e: e.tensor_scalar(
                    wnb[:], selb[:], r8[:], None, ALU.mult), waits=w)
                w = [twn, tok_consts]
                if prev_wtev:
                    w.append(prev_wtev)
                ttw = P.step("tensor", lambda e: e.transpose(psWT[:], wnb[:], ident[:]), waits=w)
                prev_ttw[0] = ttw
                twte = P.step("vector", lambda e, tt=tt: e.tensor_copy(
                    wtsb[:, tt * 128:(tt + 1) * 128], psWT[:]), waits=[ttw])
                prev_wtev = twte

            stage1(0)
            for tt in range(NT):
                if tt + 1 < NT:
                    stage1(tt + 1)
                stage2(tt)
            P.step("sync", lambda e: e.dma_start(WTscr, wtsb[:]), waits=[prev_wtev], n_dma=1)

    ot_stack.close()
    qt_stack.close()
    P.barrier()

    if dbg:
        P.step("sync", lambda e: [
            e.dma_start(dbg_x1, X1scr),
            e.dma_start(dbg_xl2, XL2scr),
            e.dma_start(dbg_wt, WTscr),
        ], n_dma=3)
        P.barrier()

    P.phase = "phSH"
    gub_stack = contextlib.ExitStack()
    qbuf = [gub_stack.enter_context(nc.sbuf_tensor(f"qbuf{i}", [128, 16, 256], BF16, side="right"))
            for i in range(3)]

    def gu_q_ap(ex, q4):
        return gu_wT[ex].rearrange("(c p) i -> p c i", p=128)[:, :, q4 * 256:(q4 + 1) * 256]

    qtok = [None, None, None]
    qfree = [None, None, None]
    for qi in range(2):
        qtok[qi] = P.step("gpsimd", lambda e, qi=qi: e.dma_start(
            qbuf[qi][:], gu_q_ap(0, qi)), n_dma=1)

    x1rr = X1scr.rearrange("(c p) m -> p c m", p=128)

    def x1_block_ap(t0):
        return X1scr[t0:t0 + TB, :].rearrange("(c p) m -> p c m", p=128)

    def out_block_ap(t0):
        return out[t0:t0 + TB, :].rearrange("(c p) m -> p c m", p=128)

    with contextlib.ExitStack() as ss:
        shgb = ss.enter_context(nc.sbuf_tensor("shgb", [128, 16, SI], BF16))
        shub = ss.enter_context(nc.sbuf_tensor("shub", [128, 16, SI], BF16))
        shdb = ss.enter_context(nc.sbuf_tensor("shdb", [128, 4, DM], BF16))
        g1b = ss.enter_context(nc.sbuf_tensor("g1b", [128, 16], BF16))
        xl2b = ss.enter_context(nc.sbuf_tensor("xl2b", [128, 16, TB], BF16))
        x1b = ss.enter_context(nc.sbuf_tensor("x1b", [128, 4, DM], F32))
        sg = ss.enter_context(nc.sbuf_tensor("sg", [128, 4, SI], F32))
        shint = ss.enter_context(nc.sbuf_tensor("shint", [128, 4, TB], BF16))
        shg = ss.enter_context(nc.sbuf_tensor("shg", [128, 4], F32))

        with contextlib.ExitStack() as sps:
            psA = [sps.enter_context(nc.psum_tensor(f"psSA{i}", [128, 512], F32)) for i in range(2)]
            psD = [sps.enter_context(nc.psum_tensor(f"psSD{i}", [128, 512], F32)) for i in range(2)]
            psG = sps.enter_context(nc.psum_tensor("psG", [128, 4], F32))

            tshw = P.step("gpsimd", lambda e: [
                e.dma_start(shgb[:], shgr),
                e.dma_start(shub[:], shur),
                e.dma_start(shdb[:], shdr),
                e.dma_start(g1b[:], g1h),
            ], n_dma=4)
            if dbg:
                tshw = P.step("sync", lambda e: e.dma_start(dbg_shgb, shgb[:]),
                              waits=[tshw], n_dma=1)

            prev_wr = None
            for tbk in range(2):
                t0 = tbk * TB
                w = [prev_wr] if prev_wr else []
                tx1 = P.step("sync", lambda e, t0=t0: [
                    e.dma_start(x1b[:], x1_block_ap(t0)),
                    e.dma_start(xl2b[:], xl2r[:, :, t0:t0 + TB]),
                ], waits=w, n_dma=2)
                prev_g = [None, None]
                sg_toks = []
                for sc in range(4):
                    gpar = sc % 2
                    w = [tx1, tshw]
                    if prev_g[gpar]:
                        w.append(prev_g[gpar])
                    tg = P.step("tensor", lambda e, sc=sc, gpar=gpar: [
                        e.matmul(psA[gpar][:], shgb[:, kc, sc * 128:(sc + 1) * 128],
                                 xl2b[:, kc, :], start=(kc == 0), stop=(kc == 15),
                                 skip_group_check=True)
                        for kc in range(16)
                    ], waits=w)
                    tsg = P.step("scalar", lambda e, sc=sc, gpar=gpar: e.activation(
                        sg[:, sc, :], psA[gpar][:], AF.Silu), waits=[tg])
                    prev_g[gpar] = tsg
                    sg_toks.append(tsg)
                up_toks = []
                for sc in range(4):
                    gpar = sc % 2
                    tu = P.step("tensor", lambda e, sc=sc, gpar=gpar: [
                        e.matmul(psA[gpar][:], shub[:, kc, sc * 128:(sc + 1) * 128],
                                 xl2b[:, kc, :], start=(kc == 0), stop=(kc == 15),
                                 skip_group_check=True)
                        for kc in range(16)
                    ], waits=[tx1, prev_g[gpar]])
                    tui = P.step("vector", lambda e, sc=sc, gpar=gpar: e.tensor_tensor(
                        shint[:, sc, :], psA[gpar][:], sg[:, sc, :], ALU.mult),
                        waits=[tu, sg_toks[sc]])
                    prev_g[gpar] = tui
                    up_toks.append(tui)
                tgl = P.step("tensor", lambda e: [
                    e.matmul(psG[:, ts_:ts_ + 1], xl2b[:, kc, ts_ * 128:(ts_ + 1) * 128],
                             g1b[:, kc:kc+1], start=(kc == 0 and ts_ == 0),
                             stop=(kc == 15 and ts_ == 3),
                             skip_group_check=True)
                    for ts_ in range(4) for kc in range(16)
                ], waits=[tx1, tshw])
                tsig = P.step("scalar", lambda e: e.activation(shg[:], psG[:], AF.Sigmoid),
                              waits=[tgl])
                if dbg:
                    P.barrier()
                    P.step("sync", lambda e, tbk=tbk: [
                        e.dma_start(dbg_sg[tbk], sg[:]),
                        e.dma_start(dbg_shint[tbk], shint[:]),
                        e.dma_start(dbg_shg[tbk], shg[:]),
                    ], n_dma=3)
                    P.barrier()
                prev_d = [up_toks[-1], up_toks[-1]]
                dn_toks = []
                for ts_ in range(4):
                    for hb in range(4):
                        dpar = (ts_ * 4 + hb) % 2
                        td = P.step("tensor", lambda e, ts_=ts_, hb=hb, dpar=dpar: [
                            e.matmul(psD[dpar][:], shint[:, sc, ts_ * 128:(ts_ + 1) * 128],
                                     shdb[:, sc, hb * 512:(hb + 1) * 512],
                                     start=(sc == 0), stop=(sc == 3), skip_group_check=True)
                            for sc in range(4)
                        ], waits=up_toks + [prev_d[dpar]])
                        tda = P.step("vector", lambda e, ts_=ts_, hb=hb, dpar=dpar:
                                     e.scalar_tensor_tensor(
                                         x1b[:, ts_, hb * 512:(hb + 1) * 512], psD[dpar][:],
                                         shg[:, ts_:ts_ + 1],
                                         x1b[:, ts_, hb * 512:(hb + 1) * 512],
                                         ALU.mult, ALU.add),
                                     waits=[td, tsig])
                        prev_d[dpar] = tda
                        dn_toks.append(tda)
                prev_wr = P.step("sync", lambda e, t0=t0: e.dma_start(
                    x1_block_ap(t0), x1b[:]), waits=dn_toks, n_dma=1)

    P.barrier()

    if dbg:
        P.step("sync", lambda e: e.dma_start(dbg_x1s, X1scr), n_dma=1)
        P.barrier()

    P.phase = "phFFN"
    with contextlib.ExitStack() as sf:
        xl2c = sf.enter_context(nc.sbuf_tensor("xl2c", [128, 16, OWN], BF16))
        x1c = sf.enter_context(nc.sbuf_tensor("x1c", [128, 8, DM], F32))
        dpb = [sf.enter_context(nc.sbuf_tensor(f"dpb{i}", [128, 4, DM], BF16)) for i in range(2)]
        intr = [sf.enter_context(nc.sbuf_tensor(f"intr{i}", [128, 4, TB], BF16)) for i in range(4)]
        silu_g = sf.enter_context(nc.sbuf_tensor("silu_g", [128, 2, 4, TB], BF16))
        tup = sf.enter_context(nc.sbuf_tensor("tup", [128, TB], F32))
        wrow = [sf.enter_context(nc.sbuf_tensor(f"wrow{i}", [1, 2, OWN], BF16)) for i in range(2)]

        with contextlib.ExitStack() as spf:
            psGa = [spf.enter_context(nc.psum_tensor(f"psGa{i}", [128, 512], F32)) for i in range(4)]
            psWb = [spf.enter_context(nc.psum_tensor(f"psWb{i}", [128, 512], F32)) for i in range(2)]
            psDn = [spf.enter_context(nc.psum_tensor(f"psDn{i}", [128, 512], F32)) for i in range(2)]

            dp_tok = [None, None]
            down_done = {}
            upev_done = {}
            gate_done = [None, None]
            up_done = [None, None]

            def dp_ap(ex):
                return dp_wT[ex].rearrange("(c p) m -> p c m", p=128)

            tx1 = P.step("sync", lambda e: [
                e.dma_start(x1c[:], X1scr.rearrange("(c p) m -> p c m", p=128)),
                e.dma_start(xl2c[:], xl2r[:, :, :]),
            ], waits=[P.tok("vector")], n_dma=2)
            wr_tok = {}
            wr_tok[0] = P.step("sync", lambda e: [
                e.dma_start(wrow[0][0:1, i, :], WTscr[i, :][None, :]) for i in range(2)
            ], waits=[P.tok("vector")], n_dma=2)
            dp_tok[0] = P.step("gpsimd", lambda e: e.dma_start(dpb[0][:], dp_ap(0)), n_dma=1)
            dp_tok[1] = P.step("gpsimd", lambda e: e.dma_start(dpb[1][:], dp_ap(1)), n_dma=1)
            bank_ev = {}

            def down_pair(g):
                last = None
                dn_prev = [None, None]
                for tbk in range(2):
                    for ts_ in range(4):
                        tile = tbk * 4 + ts_
                        for hb in range(4):
                            dpar = (ts_ * 4 + hb) % 2
                            w = [upev_done[(2 * g) * 2 + tbk], upev_done[(2 * g + 1) * 2 + tbk],
                                 dp_tok[0], dp_tok[1]]
                            if dn_prev[dpar]:
                                w.append(dn_prev[dpar])
                            td = P.step("tensor", lambda e, g=g, tbk=tbk, ts_=ts_, hb=hb, dpar=dpar: [
                                e.matmul(psDn[dpar][:],
                                         intr[ee * 2 + tbk][:, ic, ts_ * 128:(ts_ + 1) * 128],
                                         dpb[ee][:, ic, hb * 512:(hb + 1) * 512],
                                         start=(ee == 0 and ic == 0), stop=(ee == 1 and ic == 3),
                                         skip_group_check=True)
                                for ee in range(2) for ic in range(4)
                            ], waits=w)
                            ta2 = P.step("vector", lambda e, tile=tile, hb=hb, dpar=dpar:
                                         e.tensor_tensor(
                                             x1c[:, tile, hb * 512:(hb + 1) * 512], psDn[dpar][:],
                                             x1c[:, tile, hb * 512:(hb + 1) * 512], ALU.add),
                                         waits=[td, tx1])
                            dn_prev[dpar] = ta2
                            last = ta2
                down_done[g] = last

            for g in range(NE // 2):
                if g + 1 < NE // 2:
                    wr_tok[g + 1] = P.step("sync", lambda e, g=g: [
                        e.dma_start(wrow[(g + 1) % 2][0:1, i, :], WTscr[2 * (g + 1) + i, :][None, :])
                        for i in range(2)
                    ], waits=[P.tok("tensor")], n_dma=2)
                for ee in range(2):
                    ex = 2 * g + ee
                    twb = {}
                    for tbk in range(2):
                        w = [wr_tok[g], tok_consts]
                        prev_wb = upev_done.get((ex - 1) * 2 + tbk)
                        if prev_wb:
                            w.append(prev_wb)
                        twb[tbk] = P.step("tensor", lambda e, g=g, ee=ee, tbk=tbk: e.matmul(
                            psWb[tbk][:], ones_row_bf[:],
                            wrow[g % 2][0:1, ee, tbk * TB:(tbk + 1) * TB],
                            start=True, stop=True, skip_group_check=True), waits=w)
                    silu_toks = {}
                    up_last = {0: None, 1: None}
                    for q4 in range(4):
                        slot = (ex * 4 + q4) % 3
                        nq = ex * 4 + q4 + 1
                        if nq < NE * 4:
                            nslot = nq % 3
                            wq_ = [t for t in [qfree[nslot]] if t]
                            qtok[nslot] = P.step("gpsimd", lambda e, nq=nq, nslot=nslot: e.dma_start(
                                qbuf[nslot][:], gu_q_ap(nq // 4, nq % 4)), waits=wq_, n_dma=1)
                        for sub in range(2):
                            ic = q4 * 2 + sub
                            for tbk in range(2):
                                t0 = tbk * TB
                                bank = (ic % 2) * 2 + tbk
                                w = [qtok[slot], tx1]
                                if bank_ev.get(bank):
                                    w.append(bank_ev[bank])
                                if g >= 1 and down_done.get(g - 1):
                                    w.append(down_done[g - 1])
                                tmm = P.step("tensor", lambda e, sub=sub, bank=bank, slot=slot, t0=t0: [
                                    e.matmul(psGa[bank][:], qbuf[slot][:, kc, sub * 128:(sub + 1) * 128],
                                             xl2c[:, kc, t0:t0 + TB], start=(kc == 0), stop=(kc == 15),
                                             skip_group_check=True)
                                    for kc in range(16)
                                ], waits=w)
                                qfree[slot] = tmm
                                if ic < 4:
                                    tev = P.step("scalar", lambda e, ic=ic, tbk=tbk, bank=bank:
                                                 e.activation(silu_g[:, tbk, ic, :], psGa[bank][:],
                                                              AF.Silu), waits=[tmm])
                                    silu_toks[(ic, tbk)] = tev
                                else:
                                    t1 = P.step("vector", lambda e, ic=ic, tbk=tbk, bank=bank:
                                                e.tensor_tensor(tup[:], psGa[bank][:],
                                                                silu_g[:, tbk, ic - 4, :], ALU.mult),
                                                waits=[tmm, silu_toks[(ic - 4, tbk)]])
                                    tev = P.step("vector", lambda e, ic=ic, ee=ee, tbk=tbk:
                                                 e.tensor_tensor(
                                                     intr[ee * 2 + tbk][:, ic - 4, :], tup[:],
                                                     psWb[tbk][:], ALU.mult),
                                                 waits=[t1, twb[tbk]])
                                    up_last[tbk] = tev
                                bank_ev[bank] = tev
                    upev_done[ex * 2 + 0] = up_last[0]
                    upev_done[ex * 2 + 1] = up_last[1]
                down_pair(g)
                for nxe in (2 * g + 2, 2 * g + 3):
                    if nxe < NE:
                        dp_tok[nxe % 2] = P.step("gpsimd", lambda e, nxe=nxe: e.dma_start(
                            dpb[nxe % 2][:], dp_ap(nxe)), waits=[down_done[g]], n_dma=1)
            P.step("sync", lambda e: e.dma_start(
                out.rearrange("(c p) m -> p c m", p=128), x1c[:]),
                waits=[down_done[NE // 2 - 1]], n_dma=1)

    gub_stack.close()

    P.step("sync", lambda e: e.nop(), waits=[P.tok("sync")], n_dma=0)

    P.check()

    with contextlib.ExitStack() as semstack:
        sems = {}
        for name in KEYS:
            sems[name] = semstack.enter_context(nc.semaphore(f"sem_{name}"))
        P.emit(nc, sems)

    gstack.close()
    return nc



def core_tiles(core):
    j = core % 4
    tiles = []
    for s in range(4):
        tiles.extend([8 * s + j, 8 * s + (7 - j)])
    return sorted(tiles)


def _host_inputs(core, full, consts):
    b = core // 4
    tiles = core_tiles(core)
    rows = np.concatenate([np.arange(128 * t, 128 * t + 128) for t in tiles])

    xb = full["xb"][b]
    x_own = np.ascontiguousarray(xb[rows])
    masks = np.ones((32, 128, 128), dtype=np.float32)
    jj = np.arange(128)[:, None]
    qq = np.arange(128)[None, :]
    tri = np.where(jj > qq, np.float32(0.0), np.float32(1.0))
    for s in range(8):
        t_s = tiles[s]
        for i in range(4):
            jc = 4 * s + i
            if jc == t_s:
                masks[jc] = tri
            elif jc > t_s:
                masks[jc] = np.float32(0.0)
    d = {
        "x_feat": full["x_feat"][b],
        "x_own_feat": np.ascontiguousarray(x_own.T),
        "x_own": x_own,
        "cosk": consts["cosk"], "sink": consts["sink"],
        "cosq": np.ascontiguousarray(consts["cosk"][:, rows]),
        "sinq": np.ascontiguousarray(consts["sink"][:, rows]),
        "masks": masks.reshape(32 * 128, 128).astype(ml_dtypes.bfloat16),
    }
    d.update(full["weights"])
    d.update(consts["small"])
    return d, b, rows


def prep_full(L0_attn_x, L0_attn_wq, L0_attn_wk, L0_attn_wv, L0_attn_wo,
              L0_ffn_gate_w, L0_ffn_gu_w, L0_ffn_dp_w, L0_ffn_sh_gate_w,
              L0_ffn_sh_up_w, L0_ffn_sh_down_w, L0_ffn_sh_gate1_w):
    x = np.asarray(L0_attn_x)
    full = {
        "xb": [np.ascontiguousarray(x[b]) for b in range(2)],
        "x_feat": [np.ascontiguousarray(x[b].T) for b in range(2)],
        "weights": {
            "wq": np.ascontiguousarray(np.asarray(L0_attn_wq).reshape(DM, NH * DH)),
            "wk": np.ascontiguousarray(np.asarray(L0_attn_wk).reshape(DM, NH * DH)),
            "wv": np.ascontiguousarray(np.asarray(L0_attn_wv).reshape(DM, NH * DH)),
            "wo": np.ascontiguousarray(np.asarray(L0_attn_wo)),
            "gate_wT": np.ascontiguousarray(np.asarray(L0_ffn_gate_w).T),
            "gu_wT": np.ascontiguousarray(np.asarray(L0_ffn_gu_w).transpose(0, 2, 1)),
            "dp_wT": np.ascontiguousarray(np.asarray(L0_ffn_dp_w).transpose(0, 2, 1)),
            "shgT": np.ascontiguousarray(np.asarray(L0_ffn_sh_gate_w).T),
            "shuT": np.ascontiguousarray(np.asarray(L0_ffn_sh_up_w).T),
            "shdT": np.ascontiguousarray(np.asarray(L0_ffn_sh_down_w).T),
            "g1h": np.ascontiguousarray(np.asarray(L0_ffn_sh_gate1_w).reshape(16, 128).T),
        },
    }
    half = 64
    pos = np.arange(T, dtype=np.float32)[:, None]
    freqs = np.power(np.float32(10000.0),
                     -(np.arange(half, dtype=np.float32) / np.float32(half)))[None, :]
    rad = pos * freqs
    mult = np.float32(np.sqrt(1.0 / 128.0))
    consts = {
        "cosk": np.ascontiguousarray((np.cos(rad) * mult).astype(np.float32).T),
        "sink": np.ascontiguousarray((np.sin(rad) * mult).astype(np.float32).T),
        "small": {
            "c_ident": np.eye(128, dtype=ml_dtypes.bfloat16),
            "c_ones_row_bf": np.ones((1, 128), dtype=ml_dtypes.bfloat16),
            "c_ones_row_f32": np.ones((1, 128), dtype=np.float32),
            "c_ones_col_bf": np.ones((128, 1), dtype=ml_dtypes.bfloat16),
        },
    }
    return full, consts


def kernel(**inputs):
    full, consts = prep_full(**inputs)
    in_maps = []
    row_maps = []
    for core in range(8):
        d, b, rows = _host_inputs(core, full, consts)
        in_maps.append(d)
        row_maps.append((b, rows))
    nc = build_program(dbg=False)
    res = run_bass_kernel_spmd(nc, in_maps, core_ids=list(range(8)))
    outp = np.empty((2, T, DM), dtype=np.float32)
    for core in range(8):
        b, rows = row_maps[core]
        outp[b][rows] = res.results[core]["out"]
    return outp
